# version 17
# baseline (speedup 1.0000x reference)
import math
from contextlib import ExitStack

import numpy as np

import concourse.bass as bass
import concourse.mybir as mybir
from concourse.bass_utils import run_bass_kernel_spmd

F32 = mybir.dt.float32
BF16 = mybir.dt.bfloat16
AF = mybir.ActivationFunctionType
ALU = mybir.AluOpType

B, L, D = 4, 2048, 2048
T = 2048
TQ = 1024
NCH = D // 128
NH = 16
KT = T // 128
DOWN, UP, RD = 512, 1024, 64
SCALE = 1.0 / math.sqrt(192.0)
EPS = 1e-5
ISQ8 = 1.0 / math.sqrt(8.0)

DEBUG = ()
STOP = None
LAST_DEBUG = {}


class Sem:
    def __init__(self, es, nc, name):
        self.h = es.enter_context(nc.semaphore(name))
        self.n = 0


def build_nc(debug=(), stop=None):
    debug = set(debug)
    skip_cc = bool(stop) and stop.endswith("0")
    if skip_cc:
        stop = stop[:-1]
    nc = bass.Bass("TRN2", target_bir_lowering=False)

    ht = nc.dram_tensor("ht", [D, T], F32, kind="ExternalInput").ap()
    wdkvT = nc.dram_tensor("wdkvT", [D, DOWN], F32, kind="ExternalInput").ap()
    wdqT = nc.dram_tensor("wdqT", [D, DOWN], F32, kind="ExternalInput").ap()
    wkr2T = nc.dram_tensor("wkr2T", [D, 128], F32, kind="ExternalInput").ap()
    wupkT = nc.dram_tensor("wupkT", [DOWN, UP], F32, kind="ExternalInput").ap()
    wupvT = nc.dram_tensor("wupvT", [DOWN, UP], F32, kind="ExternalInput").ap()
    wupqT = nc.dram_tensor("wupqT", [DOWN, UP], F32, kind="ExternalInput").ap()
    wqrTp = nc.dram_tensor("wqrTp", [DOWN, UP], F32, kind="ExternalInput").ap()
    wfcTe = nc.dram_tensor("wfcTe", [128, NCH * 1024], F32, kind="ExternalInput").ap()
    bdkv = nc.dram_tensor("bdkv", [128, 4], F32, kind="ExternalInput").ap()
    bdq = nc.dram_tensor("bdq", [128, 4], F32, kind="ExternalInput").ap()
    bkr2 = nc.dram_tensor("bkr2", [128, 1], F32, kind="ExternalInput").ap()
    bupk = nc.dram_tensor("bupk", [128, 8], F32, kind="ExternalInput").ap()
    bupv = nc.dram_tensor("bupv", [UP], F32, kind="ExternalInput").ap()
    bupq = nc.dram_tensor("bupq", [128, 8], F32, kind="ExternalInput").ap()
    bqrp = nc.dram_tensor("bqrp", [128, 8], F32, kind="ExternalInput").ap()
    bfc = nc.dram_tensor("bfc", [128, 16], F32, kind="ExternalInput").ap()
    bffn = nc.dram_tensor("bffn", [128, 16], F32, kind="ExternalInput").ap()
    ktabc = nc.dram_tensor("ktabc", [128, T], F32, kind="ExternalInput").ap()
    ktabs = nc.dram_tensor("ktabs", [128, T], F32, kind="ExternalInput").ap()
    psgn = nc.dram_tensor("psgn", [128, 128], F32, kind="ExternalInput").ap()
    factc = nc.dram_tensor("factc", [D, D], F32, kind="ExternalInput").ap()

    out_t = nc.dram_tensor("out_t", [D, TQ], F32, kind="ExternalOutput").ap()

    dbg = {}

    def dbg_out(name, shape, dtype=BF16):
        if name in debug:
            dbg[name] = nc.dram_tensor(name, shape, dtype, kind="ExternalOutput").ap()
            return dbg[name]
        return None

    fact_in = nc.dram_tensor("fact_in", [D, D], F32)
    fact_sum = nc.dram_tensor("fact_sum", [D, D], F32)

    es = ExitStack()
    with es:
        sLB = Sem(es, nc, "sLB")
        sLM = Sem(es, nc, "sLM")
        sLH = Sem(es, nc, "sLH")
        sLT = Sem(es, nc, "sLT")
        sLW = Sem(es, nc, "sLW")
        sLU = Sem(es, nc, "sLU")
        sLF = Sem(es, nc, "sLF")
        sGX = Sem(es, nc, "sGX")
        sGF = Sem(es, nc, "sGF")
        sGD = Sem(es, nc, "sGD")
        sCC = Sem(es, nc, "sCC")
        sMM = Sem(es, nc, "sMM")
        sSC = Sem(es, nc, "sSC")
        sVE = Sem(es, nc, "sVE")
        sST = Sem(es, nc, "sST")

        blk = es.enter_context(nc.Block())

        pend = {"te": [], "sc": [], "ve": [], "sy": [], "gp": []}

        def EM(key, fn):
            pend[key].append(fn)

        def W(key, sem, val):
            if val > 0:
                h = sem.h
                pend[key].append(lambda e: e.wait_ge(h, val))

        def I(key, sem, fn, k=1):
            sem.n += k
            tgt = sem.n
            h = sem.h
            pend[key].append(lambda e: fn(e).then_inc(h, k))
            return tgt

        def flush():
            for key, attr in (("te", "tensor"), ("sc", "scalar"), ("ve", "vector"),
                              ("sy", "sync"), ("gp", "gpsimd")):
                fns = pend[key]
                if not fns:
                    continue
                pend[key] = []
                dec = getattr(blk, attr)

                def _section(e, fns=fns):
                    for f in fns:
                        f(e)
                dec(_section)

        ones_c16 = es.enter_context(nc.sbuf_tensor("ones_c16", [128, 1], BF16))
        ones_r16 = es.enter_context(nc.sbuf_tensor("ones_r16", [1, 128], BF16))
        ones_rf = es.enter_context(nc.sbuf_tensor("ones_rf", [128, 128], F32))
        ROWS = es.enter_context(nc.sbuf_tensor("ROWS", [128, 3072], F32))
        negfs16 = es.enter_context(nc.sbuf_tensor("negfs16", [1, D], BF16))
        m3r16 = es.enter_context(nc.sbuf_tensor("m3r16", [1, TQ], BF16))
        rz16 = es.enter_context(nc.sbuf_tensor("rz16", [1, TQ], BF16))
        ktabc16 = es.enter_context(nc.sbuf_tensor("ktabc16", [128, T], BF16))
        ktabs16 = es.enter_context(nc.sbuf_tensor("ktabs16", [128, T], BF16))
        psgn16 = es.enter_context(nc.sbuf_tensor("psgn16", [128, 128], BF16))
        bdkv_sb = es.enter_context(nc.sbuf_tensor("bdkv_sb", [128, 4], F32))
        bdq_sb = es.enter_context(nc.sbuf_tensor("bdq_sb", [128, 4], F32))
        bkr2_sb = es.enter_context(nc.sbuf_tensor("bkr2_sb", [128, 1], F32))
        bupk_sb = es.enter_context(nc.sbuf_tensor("bupk_sb", [128, 8], F32))
        bupq_sb = es.enter_context(nc.sbuf_tensor("bupq_sb", [128, 8], F32))
        bqrp_sb = es.enter_context(nc.sbuf_tensor("bqrp_sb", [128, 8], F32))
        bfc_sb = es.enter_context(nc.sbuf_tensor("bfc_sb", [128, 16], F32))
        bffn_sb = es.enter_context(nc.sbuf_tensor("bffn_sb", [128, 16], F32))
        bupv_r16 = es.enter_context(nc.sbuf_tensor("bupv_r16", [1, UP], BF16))
        scrA = es.enter_context(nc.sbuf_tensor("scrA", [128, 2048], BF16))
        scrB = es.enter_context(nc.sbuf_tensor("scrB", [128, 2048], BF16))

        st1 = ExitStack()
        ht16 = st1.enter_context(nc.sbuf_tensor("ht16", [128, NCH * T], BF16))
        wdkvT16 = st1.enter_context(nc.sbuf_tensor("wdkvT16", [128, NCH * DOWN], BF16))
        wdqT16 = st1.enter_context(nc.sbuf_tensor("wdqT16", [128, NCH * DOWN], BF16))
        wkr2T16 = st1.enter_context(nc.sbuf_tensor("wkr2T16", [128, NCH * 128], BF16))
        stLN = ExitStack()
        r_bc = stLN.enter_context(nc.sbuf_tensor("r_bc", [128, T], BF16))
        mrow16 = stLN.enter_context(nc.sbuf_tensor("mrow16", [1, T], BF16))
        negw16 = stLN.enter_context(nc.sbuf_tensor("negw16", [1, 2 * DOWN + 128], BF16))
        htf = stLN.enter_context(nc.sbuf_tensor("htf", [128, 2 * T], F32))

        st15 = ExitStack()
        ckv16 = st15.enter_context(nc.sbuf_tensor("ckv16", [128, 4 * T], BF16, side="right"))
        kr_rope16 = st15.enter_context(nc.sbuf_tensor("kr_rope16", [128, T], BF16, side="right"))
        stCQ = ExitStack()
        cq16 = stCQ.enter_context(nc.sbuf_tensor("cq16", [128, 4 * TQ], BF16, side="right"))
        wqr_st = ExitStack()
        wqrT16 = wqr_st.enter_context(nc.sbuf_tensor("wqrT16", [128, 4 * UP], BF16, side="right"))
        kr_st = ExitStack()
        kr_raw16 = kr_st.enter_context(nc.sbuf_tensor("kr_raw16", [128, T], BF16, side="right"))

        for bsrc, bdst in ((bdkv, bdkv_sb), (bdq, bdq_sb), (bupk, bupk_sb),
                           (bupq, bupq_sb), (bqrp, bqrp_sb), (bfc, bfc_sb),
                           (bffn, bffn_sb), (bkr2, bkr2_sb)):
            I("sy", sLM, lambda e, a=bsrc, b=bdst: e.dma_start(out=b[:], in_=a), 16)
        LM_ALL = sLM.n
        if not skip_cc:
            I("sy", sLB, lambda e: e.dma_start(out=fact_in.ap(), in_=factc), 16)

        ve_base = sVE.n + 3
        conv = [ve_base + k + 1 for k in range(NCH)]
        ht_ld = [0] * NCH
        for k in range(NCH):
            if k >= 2:
                W("sy", sVE, conv[k - 2])
            ht_ld[k] = I("sy", sLH, lambda e, k=k: e.dma_start(
                out=htf[:, (k % 2) * T:(k % 2 + 1) * T],
                in_=ht[k * 128:(k + 1) * 128, :]), 16)
        for w, t, nco in ((wdkvT, wdkvT16, DOWN), (wdqT, wdqT16, DOWN), (wkr2T, wkr2T16, 128)):
            for k in range(NCH):
                I("gp", sLW, lambda e, w=w, t=t, nco=nco, k=k: e.dma_start(
                    out=t[:, k * nco:(k + 1) * nco], in_=w[k * 128:(k + 1) * 128, :]), 16)
        I("gp", sLT, lambda e: e.dma_start(out=ktabc16[:], in_=ktabc), 16)
        I("gp", sLT, lambda e: e.dma_start(out=ktabs16[:], in_=ktabs), 16)
        I("gp", sLT, lambda e: e.dma_start(out=psgn16[:], in_=psgn), 16)
        I("gp", sLT, lambda e: e.dma_start(out=bupv_r16[0:1, :], in_=bupv[None, :]), 16)
        for k in range(4):
            I("gp", sLU, lambda e, k=k: e.dma_start(
                out=wqrT16[:, k * UP:(k + 1) * UP], in_=wqrTp[k * 128:(k + 1) * 128, :]), 16)
        LU_QR = sLU.n
        LT_ALL = sLT.n
        LW_ALL = sLW.n

        EM("ve", lambda e: e.memset(ones_c16[:], 1.0))
        EM("ve", lambda e: e.memset(ones_r16[:], 1.0))
        EM("ve", lambda e: e.memset(ones_rf[:], 1.0))
        sVE.n += 3
        ve_consts = sVE.n
        pend["ve"][-1] = (lambda fn: (lambda e: fn(e).then_inc(sVE.h, 3)))(pend["ve"][-1])
        assert ve_consts == ve_base
        for k in range(NCH):
            W("ve", sLH, ht_ld[k])
            got = I("ve", sVE, lambda e, k=k: e.tensor_copy(
                ht16[:, k * T:(k + 1) * T], htf[:, (k % 2) * T:(k % 2 + 1) * T]))
            assert got == conv[k]
        LH_ALL = sLH.n

        p1 = ExitStack()
        ST_X = p1.enter_context(nc.psum_tensor("ST_X", [1, T], F32))
        ST_S = p1.enter_context(nc.psum_tensor("ST_S", [1, T], F32))

        W("sc", sLM, LM_ALL)
        W("te", sVE, ve_consts)
        sq_done = [0] * NCH
        stat_done = [0] * NCH
        for k in range(NCH):
            W("sc", sVE, conv[k])
            if k >= 2:
                W("sc", sMM, stat_done[k - 2])
            sq_done[k] = I("sc", sSC, lambda e, k=k: e.square(
                (scrA if k % 2 == 0 else scrB)[:], ht16[:, k * T:(k + 1) * T]))
            W("te", sSC, sq_done[k])

            def stats_k(e, k=k):
                tgt = scrA if k % 2 == 0 else scrB
                for s in range(4):
                    e.matmul(ST_X[0:1, s * 512:(s + 1) * 512], ones_c16[:, 0:1],
                             ht16[:, k * T + s * 512: k * T + (s + 1) * 512],
                             start=(k == 0), stop=(k == NCH - 1))
                    mm = e.matmul(ST_S[0:1, s * 512:(s + 1) * 512], ones_c16[:, 0:1],
                                  tgt[:, s * 512:(s + 1) * 512],
                                  start=(k == 0), stop=(k == NCH - 1))
                return mm
            stat_done[k] = I("te", sMM, stats_k)
        mm_stats1 = sMM.n

        def emit_ln_rows(n_tok, wait_mm, stx, sts, scr_row=None):
            SCR = scr_row if scr_row is not None else ROWS[0:1, 2048:2048 + n_tok]
            W("sc", sMM, wait_mm)
            sc_rows = I("sc", sSC, lambda e: e.mul(ROWS[0:1, 0:n_tok], stx[0:1, 0:n_tok], 1.0 / D))
            W("ve", sSC, sc_rows)
            EM("ve", lambda e: e.tensor_tensor(SCR, ROWS[0:1, 0:n_tok], ROWS[0:1, 0:n_tok], op=ALU.mult))
            EM("ve", lambda e: e.scalar_tensor_tensor(SCR, sts[0:1, 0:n_tok], 1.0 / D, SCR,
                                                      ALU.mult, ALU.subtract))
            ve_var = I("ve", sVE, lambda e: e.tensor_scalar_add(SCR, SCR, EPS))
            W("sc", sVE, ve_var)
            EM("sc", lambda e: e.activation(ROWS[32:33, 0:n_tok], SCR, AF.Ln))
            return I("sc", sSC, lambda e: e.activation(
                ROWS[64:65, 0:n_tok], ROWS[32:33, 0:n_tok], AF.Exp, scale=-0.5))

        sc_rstd1 = emit_ln_rows(T, mm_stats1, ST_X, ST_S, scr_row=htf[0:1, 0:T])
        p1.close()

        p2 = ExitStack()
        BC2 = p2.enter_context(nc.psum_tensor("BC2", [128, T], F32))
        WS = p2.enter_context(nc.psum_tensor("WS", [1, 2 * DOWN + 128], F32))

        W("te", sSC, sc_rstd1)

        def bc1_mm(e):
            for s in range(4):
                mm = e.matmul(BC2[:, s * 512:(s + 1) * 512], ones_rf[64:65, :],
                              ROWS[64:65, s * 512:(s + 1) * 512], start=True, stop=True)
            return mm
        mm_bc1 = I("te", sMM, bc1_mm)

        def ws_mm(e):
            for k in range(NCH):
                e.matmul(WS[0:1, 0:DOWN], ones_c16[:, 0:1],
                         wdkvT16[:, k * DOWN:(k + 1) * DOWN],
                         start=(k == 0), stop=(k == NCH - 1))
                e.matmul(WS[0:1, DOWN:2 * DOWN], ones_c16[:, 0:1],
                         wdqT16[:, k * DOWN:(k + 1) * DOWN],
                         start=(k == 0), stop=(k == NCH - 1))
                mm = e.matmul(WS[0:1, 2 * DOWN:2 * DOWN + 128], ones_c16[:, 0:1],
                              wkr2T16[:, k * 128:(k + 1) * 128],
                              start=(k == 0), stop=(k == NCH - 1))
            return mm
        W("te", sLW, LW_ALL)
        mm_ws = I("te", sMM, ws_mm)

        W("sc", sMM, mm_bc1)

        def bc1_cp(e):
            for s in range(4):
                e.copy(r_bc[:, s * 512:(s + 1) * 512], BC2[:, s * 512:(s + 1) * 512])
            e.copy(mrow16[0:1, :], ROWS[0:1, 0:T])
            return e.mul(negw16[0:1, :], WS[0:1, :], -1.0)
        W("sc", sMM, mm_ws)
        sc_bc1 = I("sc", sSC, bc1_cp)
        p2.close()
        ve_xs = sVE.n

        def _stop_here(stacks=()):
            W("gp", sGD, sGD.n)
            flush()
            for st in stacks:
                st.close()
            return nc, sorted(dbg.keys())

        if dbg_out("d_xs", [128, T]):
            W("gp", sVE, ve_xs)
            I("gp", sGD, lambda e: e.dma_start(out=dbg["d_xs"], in_=ht16[:, 0:T]), 16)
        if stop == "xs":
            return _stop_here([stLN, st1, kr_st, stCQ, st15])

        p3 = ExitStack()
        PR = [p3.enter_context(nc.psum_tensor(f"PR{i}", [128, T], F32)) for i in range(2)]
        pr_tick = [0]
        pr_ev = []

        def emit_proj(wt16, wt_cols, n_co, n_tok, nw_off, out16, bias_sb, first=None, chunk_gate=None):
            evs = []
            for j in range(n_co):
                tick = pr_tick[0]
                pj = PR[tick % 2]
                if first is not None and j == 0:
                    first()
                if tick >= 2:
                    W("te", sVE, pr_ev[tick - 2])
                for k in range(NCH):
                    if chunk_gate is not None and j == 0:
                        W("ve" if False else "te", sVE, chunk_gate[k])

                    def mmk(e, j=j, pj=pj, wt16=wt16, wt_cols=wt_cols, n_tok=n_tok, k=k):
                        for s in range(n_tok // 512):
                            mm = e.matmul(
                                pj[:, s * 512:(s + 1) * 512],
                                wt16[:, k * wt_cols + j * 128: k * wt_cols + j * 128 + 128],
                                ht16[:, k * T + s * 512: k * T + (s + 1) * 512],
                                start=(k == 0), stop=False)
                        return mm
                    EM("te", mmk)

                def rank1(e, j=j, pj=pj, n_tok=n_tok, nw_off=nw_off):
                    for s in range(n_tok // 512):
                        mm = e.matmul(
                            pj[:, s * 512:(s + 1) * 512],
                            negw16[0:1, nw_off + j * 128: nw_off + j * 128 + 128],
                            mrow16[0:1, s * 512:(s + 1) * 512],
                            start=False, stop=True)
                    return mm
                mmn = I("te", sMM, rank1)
                W("ve", sMM, mmn)
                EM("ve", lambda e, j=j, pj=pj, n_tok=n_tok: e.tensor_tensor(
                    scrA[:, 0:n_tok], pj[:, 0:n_tok], r_bc[:, 0:n_tok], op=ALU.mult))
                ev = I("ve", sVE, lambda e, j=j, n_tok=n_tok: e.tensor_scalar_add(
                    out16[:, j * n_tok:(j + 1) * n_tok], scrA[:, 0:n_tok],
                    bias_sb[:, j:j + 1]))
                pr_ev.append(ev)
                evs.append(ev)
                pr_tick[0] += 1
            return evs

        def _first_ckv():
            W("te", sLW, LW_ALL)
            W("te", sSC, sc_bc1)

        emit_proj(wdkvT16, DOWN, 4, T, 0, ckv16, bdkv_sb,
                  first=_first_ckv, chunk_gate=conv)
        ckv_ev_last = pr_ev[-1]
        emit_proj(wkr2T16, 128, 1, T, 2 * DOWN, kr_raw16, bkr2_sb)
        kr_raw_ev = pr_ev[-1]
        emit_proj(wdqT16, DOWN, 4, TQ, DOWN, cq16, bdq_sb)
        ve_down_done = pr_ev[-1]

        if dbg_out("d_ckv", [128, 4 * T]):
            W("gp", sVE, ckv_ev_last)
            I("gp", sGD, lambda e: e.dma_start(out=dbg["d_ckv"], in_=ckv16[:]), 16)

        kr_tick = pr_tick[0]
        PRk = PR[kr_tick % 2]
        W("te", sVE, kr_raw_ev)
        W("te", sVE, ve_down_done)
        W("te", sLT, LT_ALL)
        W("ve", sLT, LT_ALL)

        def ksw_mm(e):
            for s in range(4):
                mm = e.matmul(PRk[:, s * 512:(s + 1) * 512], psgn16[:, :],
                              kr_raw16[:, s * 512:(s + 1) * 512], start=True, stop=True)
            return mm
        mm_ksw = I("te", sMM, ksw_mm)

        W("ve", sMM, mm_ksw)
        EM("ve", lambda e: e.tensor_tensor(scrA[:], kr_raw16[:], ktabc16[:], op=ALU.mult))
        EM("ve", lambda e: e.tensor_tensor(scrB[:], PRk[:], ktabs16[:], op=ALU.mult))
        ve_krope = I("ve", sVE, lambda e: e.tensor_tensor(kr_rope16[:], scrA[:], scrB[:], op=ALU.add))
        pr_ev.append(ve_krope)
        pr_tick[0] += 1

        if dbg_out("d_krope", [128, T]):
            W("gp", sVE, ve_krope)
            I("gp", sGD, lambda e: e.dma_start(out=dbg["d_krope"], in_=kr_rope16[:]), 16)

        kr_st.close()
        mm_down_done = sMM.n
        stLN.close()
        st1.close()
        if stop == "proj":
            return _stop_here([stCQ, st15])

        ost = ExitStack()
        o_raw = ost.enter_context(nc.sbuf_tensor("o_raw", [128, 8 * TQ], BF16))
        wup = ExitStack()
        wupkT16 = wup.enter_context(nc.sbuf_tensor("wupkT16", [128, 4 * UP], BF16))
        wupvT16 = wup.enter_context(nc.sbuf_tensor("wupvT16", [128, 4 * UP], BF16))
        qst = ExitStack()
        q_all = qst.enter_context(nc.sbuf_tensor("q_all", [128, NH * TQ], BF16))
        qrst = ExitStack()
        qr16 = qrst.enter_context(nc.sbuf_tensor("qr16", [128, 8 * TQ], BF16))
        wupq_st = ExitStack()
        wupqT16 = wupq_st.enter_context(nc.sbuf_tensor("wupqT16", [128, 4 * UP], BF16))

        W("gp", sMM, mm_down_done)
        W("gp", sVE, ve_down_done)
        for w, t in ((wupqT, wupqT16), (wupkT, wupkT16), (wupvT, wupvT16)):
            for k in range(4):
                I("gp", sLU, lambda e, w=w, t=t, k=k: e.dma_start(
                    out=t[:, k * UP:(k + 1) * UP], in_=w[k * 128:(k + 1) * 128, :]), 16)
        LU_ALL = sLU.n

        if not skip_cc:
            W("gp", sLB, 16)
            I("gp", sCC, lambda e: e.collective_compute(
                "AllReduce", ALU.add,
                replica_groups=[list(range(8))],
                ins=[fact_in.ap().opt()],
                outs=[fact_sum.ap().opt()],
            ))

        def emit_up_q(wt16, evac_fn, first=None):
            for j in range(8):
                tick = pr_tick[0]
                pj = PR[tick % 2]
                if first is not None and j == 0:
                    first()
                if tick >= 2:
                    W("te", sVE, pr_ev[tick - 2])

                def mmf(e, j=j, pj=pj, wt16=wt16):
                    for c in range(4):
                        for s in range(2):
                            mm = e.matmul(
                                pj[:, s * 512:(s + 1) * 512],
                                wt16[:, c * UP + j * 128: c * UP + j * 128 + 128],
                                cq16[:, c * TQ + s * 512: c * TQ + (s + 1) * 512],
                                start=(c == 0), stop=(c == 3))
                    return mm
                mmn = I("te", sMM, mmf)
                W("ve", sMM, mmn)
                pr_ev.append(evac_fn(j, pj))
                pr_tick[0] += 1

        def _qr_first():
            W("te", sLU, LU_QR)
            W("te", sVE, ve_down_done)

        emit_up_q(wqrT16,
                  lambda j, pj: I("ve", sVE, lambda e, j=j, pj=pj: e.tensor_scalar_add(
                      qr16[:, j * TQ:(j + 1) * TQ], pj[:, 0:TQ], bqrp_sb[:, j:j + 1])),
                  first=_qr_first)

        def qc_evac(j, pj):
            EM("ve", lambda e, j=j, pj=pj: e.tensor_scalar_add(
                q_all[0:64, (2 * j) * TQ:(2 * j + 1) * TQ],
                pj[0:64, 0:TQ], bupq_sb[0:64, j:j + 1]))
            return I("ve", sVE, lambda e, j=j, pj=pj: e.tensor_scalar_add(
                q_all[64:128, (2 * j + 1) * TQ:(2 * j + 2) * TQ],
                pj[64:128, 0:TQ], bupq_sb[64:128, j:j + 1]))

        emit_up_q(wupqT16, qc_evac,
                  first=lambda: W("te", sLU, LU_QR + 64))

        for j in range(8):
            tick = pr_tick[0]
            pj = PR[tick % 2]
            if tick >= 2:
                W("te", sVE, pr_ev[tick - 2])

            def qsw_mm(e, j=j, pj=pj):
                for s in range(2):
                    mm = e.matmul(pj[:, s * 512:(s + 1) * 512], psgn16[:, :],
                                  qr16[:, j * TQ + s * 512: j * TQ + (s + 1) * 512],
                                  start=True, stop=True)
                return mm
            mm_qsw = I("te", sMM, qsw_mm)

            W("ve", sMM, mm_qsw)
            EM("ve", lambda e, j=j: e.tensor_tensor(
                scrA[:, 0:TQ], qr16[:, j * TQ:(j + 1) * TQ], ktabc16[:, 0:TQ], op=ALU.mult))
            EM("ve", lambda e, pj=pj: e.tensor_tensor(
                scrB[:, 0:TQ], pj[:, 0:TQ], ktabs16[:, 0:TQ], op=ALU.mult))
            EM("ve", lambda e, j=j: e.tensor_tensor(
                q_all[0:64, (2 * j + 1) * TQ:(2 * j + 2) * TQ],
                scrA[0:64, 0:TQ], scrB[0:64, 0:TQ], op=ALU.add))
            qv = I("ve", sVE, lambda e, j=j: e.tensor_tensor(
                q_all[64:128, (2 * j) * TQ:(2 * j + 1) * TQ],
                scrA[64:128, 0:TQ], scrB[64:128, 0:TQ], op=ALU.add))
            pr_ev.append(qv)
            pr_tick[0] += 1
        ve_qall_done = sVE.n

        p3.close()
        wupq_st.close()
        qrst.close()
        wqr_st.close()
        stCQ.close()

        if dbg_out("d_qall", [128, NH * TQ]):
            W("gp", sVE, ve_qall_done)
            I("gp", sGD, lambda e: e.dma_start(out=dbg["d_qall"], in_=q_all[:]), 16)
        if stop == "q":
            return _stop_here([qst, wup, ost, st15])

        katt = ExitStack()
        k_all = katt.enter_context(nc.sbuf_tensor("k_all", [128, 8 * T], BF16))
        v_aug = katt.enter_context(nc.sbuf_tensor("v_aug", [128, 8 * KT * 65], BF16))
        p_buf = [katt.enter_context(nc.sbuf_tensor(f"p_buf{i}", [128, TQ], BF16)) for i in range(4)]
        o_unf = [katt.enter_context(nc.sbuf_tensor(f"o_unf{i}", [128, TQ], F32)) for i in range(2)]

        exp_hist = []
        pv_hist = []
        onorm_hist = []
        group_att_mm = [0, 0]
        group_ve_done = [0, 0]

        for g in range(2):
            pk = ExitStack()
            PK = [pk.enter_context(nc.psum_tensor(f"PK{g}{i}", [128, T], F32)) for i in range(2)]

            if g == 1:
                W("ve", sMM, group_att_mm[0])
                W("ve", sVE, group_ve_done[0])
            ve_vones = I("ve", sVE, lambda e: e.memset(v_aug[:], 1.0))

            kc_ev = [0] * 4
            for j in range(4):
                if g == 0 and j == 0:
                    W("te", sLU, LU_ALL)
                    W("te", sVE, ckv_ev_last)
                    W("te", sVE, ve_qall_done)
                if j >= 2:
                    W("te", sVE, kc_ev[j - 2])
                elif g == 1:
                    W("te", sVE, group_ve_done[0])

                def kc_mm(e, j=j, g=g, PK=PK):
                    for c in range(4):
                        for s in range(4):
                            mm = e.matmul(
                                PK[j % 2][:, s * 512:(s + 1) * 512],
                                wupkT16[:, c * UP + (4 * g + j) * 128: c * UP + (4 * g + j) * 128 + 128],
                                ckv16[:, c * T + s * 512: c * T + (s + 1) * 512],
                                start=(c == 0), stop=(c == 3))
                    return mm
                mmn = I("te", sMM, kc_mm)

                W("ve", sMM, mmn)
                EM("ve", lambda e, j=j, g=g, PK=PK: e.tensor_scalar_add(
                    k_all[0:64, (2 * j) * T:(2 * j + 1) * T],
                    PK[j % 2][0:64, :], bupk_sb[0:64, 4 * g + j:4 * g + j + 1]))
                kc_ev[j] = I("ve", sVE, lambda e, j=j, g=g, PK=PK: e.tensor_scalar_add(
                    k_all[64:128, (2 * j + 1) * T:(2 * j + 2) * T],
                    PK[j % 2][64:128, :], bupk_sb[64:128, 4 * g + j:4 * g + j + 1]))
            kc_done = sVE.n

            W("ve", sVE, ve_krope)
            for hl in range(7):
                if hl % 2 == 0:
                    EM("ve", lambda e, hl=hl: e.tensor_copy(
                        k_all[64:128, hl * T:(hl + 1) * T], kr_rope16[64:128, :]))
                else:
                    EM("ve", lambda e, hl=hl: e.tensor_copy(
                        k_all[0:64, hl * T:(hl + 1) * T], kr_rope16[0:64, :]))
            kr_copy_done = I("ve", sVE, lambda e: e.tensor_copy(
                k_all[0:64, 7 * T:8 * T], kr_rope16[0:64, :]))

            pk.close()
            pv = ExitStack()
            PV1 = [pv.enter_context(nc.psum_tensor(f"PV{g}{i}", [128, 512], F32)) for i in range(2)]
            v_ev = [0] * KT
            for t in range(KT):
                if t == 0:
                    W("te", sVE, kc_done)
                    W("te", sVE, ve_vones)
                if t >= 2:
                    W("te", sVE, v_ev[t - 2])

                def v_mmf(e, t=t, g=g, PV1=PV1):
                    for c in range(4):
                        e.matmul(PV1[t % 2][:, 0:512],
                                 ckv16[:, c * T + t * 128: c * T + t * 128 + 128],
                                 wupvT16[:, c * UP + g * 512: c * UP + (g + 1) * 512],
                                 start=(c == 0), stop=False)
                    return e.matmul(PV1[t % 2][:, 0:512], ones_r16[0:1, :],
                                    bupv_r16[0:1, g * 512:(g + 1) * 512],
                                    start=False, stop=True)
                v_mmn = I("te", sMM, v_mmf)

                W("ve", sMM, v_mmn)
                v_ev[t] = I("ve", sVE, lambda e, t=t, PV1=PV1: e.tensor_copy(
                    v_aug[:].rearrange("p (h c m) -> p h c m", h=8, c=KT)[:, :, t, 0:64],
                    PV1[t % 2][:, 0:512].rearrange("p (h m) -> p h m", h=8)))
            pv.close()
            asm_gate = sVE.n

            if g == 0 and dbg_out("d_kall", [128, 8 * T]):
                W("gp", sVE, asm_gate)
                I("gp", sGD, lambda e: e.dma_start(out=dbg["d_kall"], in_=k_all[:]), 16)
            if g == 0 and dbg_out("d_vaug", [128, 8 * KT * 65]):
                W("gp", sVE, asm_gate)
                I("gp", sGD, lambda e: e.dma_start(out=dbg["d_vaug"], in_=v_aug[:]), 16)

            patt = ExitStack()
            SP = [patt.enter_context(nc.psum_tensor(f"SP{g}{i}", [128, TQ], F32)) for i in range(2)]
            O_ps = patt.enter_context(nc.psum_tensor(f"O{g}", [128, TQ], F32))
            RZ = patt.enter_context(nc.psum_tensor(f"RZ{g}", [64, TQ], F32))

            rz_sc = [None]
            rz_te = [None]
            onorm_tt = [0]

            def emit_rz_sc():
                if rz_sc[0] is None:
                    return
                ou, ouc, dst = rz_sc[0]
                rz_sc[0] = None
                W("sc", sVE, ouc)
                EM("sc", lambda e, ou=ou: e.activation(
                    ROWS[0:1, 2048:2048 + TQ], ou[64:65, :], AF.Ln))
                sc_rz1 = I("sc", sSC, lambda e: e.activation(
                    rz16[0:1, :], ROWS[0:1, 2048:2048 + TQ], AF.Exp, scale=-1.0))
                rz_te[0] = (ou, sc_rz1, dst, onorm_tt[0])

            def emit_rz_te(exp_c4):
                if rz_te[0] is None:
                    return
                ou, sc_rz1, dst, prev_onorm = rz_te[0]
                rz_te[0] = None
                W("te", sSC, sc_rz1)
                if prev_onorm:
                    W("te", sVE, prev_onorm)

                def rz_mm(e, RZ=RZ):
                    for s in range(2):
                        mm = e.matmul(RZ[0:64, s * 512:(s + 1) * 512], ones_r16[0:1, 0:64],
                                      rz16[0:1, s * 512:(s + 1) * 512], start=True, stop=True)
                    return mm
                mm_rz = I("te", sMM, rz_mm)
                W("ve", sMM, mm_rz)
                onorm_tt[0] = I("ve", sVE, lambda e, dst=dst, ou=ou, RZ=RZ: e.tensor_tensor(
                    dst, ou[0:64, :], RZ[0:64, :], op=ALU.mult))

            LEAD = 2
            G0 = 16 * 8 * g

            def pv_emit(i):
                hl_, c = (i - G0) // 16, (i - G0) % 16
                W("te", sSC, exp_hist[i])
                if c == 0 and onorm_hist:
                    W("te", sVE, onorm_hist[-1])

                def f(e, c=c, hl_=hl_):
                    for s in range(2):
                        mm = e.matmul(
                            O_ps[0:65, s * 512:(s + 1) * 512],
                            v_aug[:, (hl_ * KT + c) * 65:(hl_ * KT + c) * 65 + 65],
                            p_buf[c % 4][:, s * 512:(s + 1) * 512],
                            start=(c == 0), stop=(c == KT - 1))
                    return mm
                pv_hist.append(I("te", sMM, f))
                if c == KT - 1:
                    hg = 8 * g + hl_
                    ou = o_unf[hl_ % 2]
                    W("ve", sMM, pv_hist[-1])
                    ouc = I("ve", sVE, lambda e, ou=ou: e.tensor_copy(
                        ou[0:65, :], O_ps[0:65, :]))
                    onorm_hist.append(ouc)
                    jj, par = hg // 2, hg % 2
                    dst = o_raw[0:64, jj * TQ:(jj + 1) * TQ] if par == 0 else \
                        o_raw[64:128, jj * TQ:(jj + 1) * TQ]
                    rz_sc[0] = (ou, ouc, dst)

            for hl in range(8):
                hglob = 8 * g + hl
                for c in range(KT):
                    i = 16 * hglob + c
                    if c == 0 and hl == 0:
                        W("te", sVE, asm_gate)
                        W("te", sVE, kr_copy_done)
                        W("te", sVE, ve_qall_done)
                    if i - G0 >= 2:
                        W("te", sSC, exp_hist[i - 2])

                    def sc_f(e, c=c, hl=hl, hglob=hglob, SP=SP):
                        for s in range(2):
                            mm = e.matmul(
                                SP[c % 2][:, s * 512:(s + 1) * 512],
                                k_all[:, hl * T + c * 128: hl * T + c * 128 + 128],
                                q_all[:, hglob * TQ + s * 512: hglob * TQ + (s + 1) * 512],
                                start=True, stop=True)
                        return mm
                    score_mm = I("te", sMM, sc_f)

                    if i - G0 >= LEAD:
                        pv_emit(i - LEAD)

                    W("sc", sMM, score_mm)
                    if i - G0 >= 4:
                        W("sc", sMM, pv_hist[i - 4])
                    exp_hist.append(I("sc", sSC, lambda e, c=c, SP=SP: e.activation(
                        p_buf[c % 4][:], SP[c % 2][:], AF.Exp, scale=SCALE)))
                    if c == 5:
                        emit_rz_sc()
                    if c == 8:
                        emit_rz_te(exp_hist[16 * hglob + 7])
            for i in range(16 * 8 * (g + 1) - LEAD, 16 * 8 * (g + 1)):
                pv_emit(i)
            emit_rz_sc()
            emit_rz_te(exp_hist[-1])
            group_ve_done[g] = sVE.n
            group_att_mm[g] = sMM.n
            patt.close()

        o_done = sVE.n
        att_free_mm = sMM.n

        if dbg_out("d_o", [128, 8 * TQ]):
            W("gp", sVE, o_done)
            I("gp", sGD, lambda e: e.dma_start(out=dbg["d_o"], in_=o_raw[:]), 16)
        if stop == "attn":
            return _stop_here([katt, qst, wup, ost, st15])

        katt.close()
        qst.close()
        wup.close()
        st15.close()

        fc_w = ExitStack()
        wfcT16 = fc_w.enter_context(nc.sbuf_tensor("wfcT16", [128, NCH * 1024], BF16))
        fc_y = ExitStack()
        y16 = fc_y.enter_context(nc.sbuf_tensor("y16", [128, NCH * TQ], BF16, side="right"))
        ffa = ExitStack()
        F16a = ffa.enter_context(nc.sbuf_tensor("F16a", [128, 8 * D], BF16, side="right"))
        fc_x = ExitStack()
        x16 = fc_x.enter_context(nc.sbuf_tensor("x16", [128, NCH * TQ], BF16, side="right"))
        fc_bc = ExitStack()
        bc2a = fc_bc.enter_context(nc.sbuf_tensor("bc2a", [128, TQ], BF16))
        bc2b = fc_bc.enter_context(nc.sbuf_tensor("bc2b", [128, TQ], BF16))

        W("gp", sMM, att_free_mm)
        W("gp", sVE, o_done)
        if not skip_cc:
            W("gp", sCC, 1)
        wfc_ld = [0] * 4
        for k in range(4):
            wfc_ld[k] = I("gp", sLF, lambda e, k=k: e.dma_start(
                out=wfcT16[:, k * 4096:(k + 1) * 4096],
                in_=wfcTe[:, k * 4096:(k + 1) * 4096]), 16)
        for k in range(NCH):
            I("gp", sGX, lambda e, k=k: e.dma_start(
                out=x16[:, k * TQ:(k + 1) * TQ], in_=ht[k * 128:(k + 1) * 128, 0:TQ]), 16)
        for k in range(8):
            I("gp", sGF, lambda e, k=k: e.dma_start(
                out=F16a[:, k * D:(k + 1) * D],
                in_=fact_sum.ap()[k * 128:(k + 1) * 128, :]), 16)
        GF_A = sGF.n
        GX_ALL = sGX.n

        pfc = ExitStack()
        PF = [pfc.enter_context(nc.psum_tensor(f"PF{i}", [128, TQ], F32)) for i in range(2)]
        ST2X = pfc.enter_context(nc.psum_tensor("ST2X", [1, TQ], F32))
        ST2S = pfc.enter_context(nc.psum_tensor("ST2S", [1, TQ], F32))

        y_ev = [0] * NCH
        y_sq = [0] * NCH
        y_stat = [0] * NCH
        y_stat_fns = [None] * NCH
        for e_ in range(NCH):
            if e_ == 0:
                W("te", sVE, o_done)
            if e_ % 4 == 0:
                W("te", sLF, wfc_ld[e_ // 4])
            if e_ >= 2:
                W("te", sVE, y_ev[e_ - 2])

            def fc_mm(e, e_=e_, PF=PF):
                for c in range(8):
                    for s in range(2):
                        mm = e.matmul(
                            PF[e_ % 2][:, s * 512:(s + 1) * 512],
                            wfcT16[:, e_ * 1024 + c * 128: e_ * 1024 + c * 128 + 128],
                            o_raw[:, c * TQ + s * 512: c * TQ + (s + 1) * 512],
                            start=(c == 0), stop=(c == 7))
                return mm
            mmfc = I("te", sMM, fc_mm)
            lag = e_ - 2
            if lag >= 0:
                W("te", sSC, y_sq[lag])
                y_stat[lag] = I("te", sMM, y_stat_fns[lag])

            W("ve", sMM, mmfc)
            y_ev[e_] = I("ve", sVE, lambda e, e_=e_, PF=PF: e.tensor_scalar_add(
                y16[:, e_ * TQ:(e_ + 1) * TQ], PF[e_ % 2][:, 0:TQ], bfc_sb[:, e_:e_ + 1]))

            W("sc", sVE, y_ev[e_])
            if e_ >= 2:
                W("sc", sMM, y_stat[e_ - 2])
            y_sq[e_] = I("sc", sSC, lambda e, e_=e_: e.square(
                (scrA if e_ % 2 == 0 else scrB)[:, 0:TQ], y16[:, e_ * TQ:(e_ + 1) * TQ]))

            def y_stat_mm(e, e_=e_):
                for s in range(2):
                    e.matmul(ST2X[0:1, s * 512:(s + 1) * 512], ones_c16[:, 0:1],
                             y16[:, e_ * TQ + s * 512: e_ * TQ + (s + 1) * 512],
                             start=(e_ == 0), stop=(e_ == NCH - 1))
                    mm = e.matmul(ST2S[0:1, s * 512:(s + 1) * 512], ones_c16[:, 0:1],
                                  (scrA if e_ % 2 == 0 else scrB)[:, s * 512:(s + 1) * 512],
                                  start=(e_ == 0), stop=(e_ == NCH - 1))
                return mm
            y_stat_fns[e_] = y_stat_mm
        for lag in (NCH - 2, NCH - 1):
            W("te", sSC, y_sq[lag])
            y_stat[lag] = I("te", sMM, y_stat_fns[lag])
        mm_stats2 = sMM.n
        mm_fc_done = mm_stats2

        if dbg_out("d_y", [128, NCH * TQ]):
            W("gp", sVE, y_ev[15])
            I("gp", sGD, lambda e: e.dma_start(out=dbg["d_y"], in_=y16[:]), 16)

        sc_rstd2 = emit_ln_rows(TQ, mm_stats2, ST2X, ST2S)
        pfc.close()

        W("ve", sSC, sc_rstd2)
        EM("ve", lambda e: e.tensor_copy(ROWS[0:1, 2048:2048 + TQ], ROWS[64:65, 0:TQ]))
        ve_mr2 = I("ve", sVE, lambda e: e.tensor_tensor(
            ROWS[32:33, 0:TQ], ROWS[0:1, 0:TQ], ROWS[0:1, 2048:2048 + TQ], op=ALU.mult))

        pbc2 = ExitStack()
        BCa = pbc2.enter_context(nc.psum_tensor("BCa", [128, TQ], F32))
        BCb = pbc2.enter_context(nc.psum_tensor("BCb", [128, TQ], F32))

        W("te", sVE, ve_mr2)

        def bc2_mm(e):
            for s in range(2):
                e.matmul(BCa[:, s * 512:(s + 1) * 512], ones_rf[64:65, :],
                         ROWS[64:65, s * 512:(s + 1) * 512], start=True, stop=True)
                mm = e.matmul(BCb[:, s * 512:(s + 1) * 512], ones_rf[32:33, :],
                              ROWS[32:33, s * 512:(s + 1) * 512], start=True, stop=True)
            return mm
        mm_bc2 = I("te", sMM, bc2_mm)

        W("sc", sMM, mm_bc2)
        EM("sc", lambda e: e.copy(bc2a[:], BCa[:]))
        sc_bc2 = I("sc", sSC, lambda e: e.copy(bc2b[:], BCb[:]))
        pbc2.close()

        x2_ve = [0] * NCH
        for e_ in range(NCH):
            if e_ == 0:
                W("ve", sSC, sc_bc2)
                W("ve", sGX, GX_ALL)
            W("ve", sSC, y_sq[e_])
            EM("ve", lambda e, e_=e_: e.tensor_tensor(
                scrA[:, 1024:1024 + TQ], y16[:, e_ * TQ:(e_ + 1) * TQ], bc2a[:], op=ALU.mult))
            EM("ve", lambda e, e_=e_: e.tensor_tensor(
                scrB[:, 1024:1024 + TQ], scrA[:, 1024:1024 + TQ], bc2b[:], op=ALU.subtract))
            x2_ve[e_] = I("ve", sVE, lambda e, e_=e_: e.tensor_tensor(
                y16[:, e_ * TQ:(e_ + 1) * TQ], x16[:, e_ * TQ:(e_ + 1) * TQ],
                scrB[:, 1024:1024 + TQ], op=ALU.add))
        ve_x2_done = x2_ve[15]

        if dbg_out("d_x2", [128, NCH * TQ]):
            W("gp", sVE, ve_x2_done)
            I("gp", sGD, lambda e: e.dma_start(out=dbg["d_x2"], in_=y16[:]), 16)
        if stop == "x2":
            return _stop_here([fc_bc, fc_x, fc_w, ost, fc_y])

        fc_bc.close()
        fc_x.close()
        fc_w.close()
        ost.close()

        ffb = ExitStack()
        F16b = ffb.enter_context(nc.sbuf_tensor("F16b", [128, 8 * D], BF16))
        ff = ExitStack()
        mixed16 = ff.enter_context(nc.sbuf_tensor("mixed16", [128, NCH * TQ], BF16, side="right"))
        bc3a = ff.enter_context(nc.sbuf_tensor("bc3a", [128, TQ], BF16, side="right"))
        bc4a = ff.enter_context(nc.sbuf_tensor("bc4a", [128, TQ], BF16, side="right"))
        bc4b = ff.enter_context(nc.sbuf_tensor("bc4b", [128, TQ], BF16, side="right"))
        outb = [ff.enter_context(nc.sbuf_tensor(f"outb{i}", [128, TQ], F32, side="right")) for i in range(2)]


        W("gp", sMM, mm_fc_done)
        for k in range(8):
            I("gp", sGF, lambda e, k=k: e.dma_start(
                out=F16b[:, k * D:(k + 1) * D],
                in_=fact_sum.ap()[(8 + k) * 128:(8 + k + 1) * 128, :]), 16)
        GF_ALL = sGF.n

        pff = ExitStack()
        ST3X = pff.enter_context(nc.psum_tensor("ST3X", [1, TQ], F32))
        ST3S = pff.enter_context(nc.psum_tensor("ST3S", [1, TQ], F32))
        ST3F = pff.enter_context(nc.psum_tensor("ST3F", [1, D], F32))

        x2_sq = [0] * NCH
        x2_stat = [0] * NCH
        for e_ in range(NCH):
            W("sc", sVE, x2_ve[e_])
            if e_ >= 2:
                W("sc", sMM, x2_stat[e_ - 2])
            x2_sq[e_] = I("sc", sSC, lambda e, e_=e_: e.square(
                (scrA if e_ % 2 == 0 else scrB)[:, 0:TQ], y16[:, e_ * TQ:(e_ + 1) * TQ]))

            W("te", sSC, x2_sq[e_])

            def x2_stat_mm(e, e_=e_):
                for s in range(2):
                    e.matmul(ST3X[0:1, s * 512:(s + 1) * 512], ones_c16[:, 0:1],
                             y16[:, e_ * TQ + s * 512: e_ * TQ + (s + 1) * 512],
                             start=(e_ == 0), stop=(e_ == NCH - 1))
                    mm = e.matmul(ST3S[0:1, s * 512:(s + 1) * 512], ones_c16[:, 0:1],
                                  (scrA if e_ % 2 == 0 else scrB)[:, s * 512:(s + 1) * 512],
                                  start=(e_ == 0), stop=(e_ == NCH - 1))
                return mm
            x2_stat[e_] = I("te", sMM, x2_stat_mm)
        mm_stats3 = sMM.n

        W("te", sGF, GF_A)

        def fsum_a(e):
            for k in range(8):
                for s in range(4):
                    mm = e.matmul(ST3F[0:1, s * 512:(s + 1) * 512], ones_c16[:, 0:1],
                                  F16a[:, k * D + s * 512: k * D + (s + 1) * 512],
                                  start=(k == 0), stop=False)
            return mm
        I("te", sMM, fsum_a)
        W("te", sGF, GF_ALL)

        def fsum_b(e):
            for k in range(8):
                for s in range(4):
                    mm = e.matmul(ST3F[0:1, s * 512:(s + 1) * 512], ones_c16[:, 0:1],
                                  F16b[:, k * D + s * 512: k * D + (s + 1) * 512],
                                  start=False, stop=(k == 7))
            return mm
        mm_fs = I("te", sMM, fsum_b)

        sc_rstd3 = emit_ln_rows(TQ, mm_stats3, ST3X, ST3S)

        EM("sc", lambda e: e.mul(ROWS[32:33, 0:TQ], ROWS[64:65, 0:TQ], ISQ8))
        W("sc", sMM, mm_fs)
        sc_negfs = I("sc", sSC, lambda e: e.mul(negfs16[0:1, 0:D], ST3F[0:1, 0:D], -1.0))
        pff.close()

        W("ve", sSC, sc_negfs)
        ve_m3r = I("ve", sVE, lambda e: e.tensor_copy(m3r16[0:1, 0:TQ], ROWS[0:1, 0:TQ]))

        pbc3 = ExitStack()
        BCc = pbc3.enter_context(nc.psum_tensor("BCc", [128, TQ], F32))

        W("te", sSC, sc_negfs)

        def bc3_mm(e):
            for s in range(2):
                mm = e.matmul(BCc[:, s * 512:(s + 1) * 512], ones_rf[32:33, :],
                              ROWS[32:33, s * 512:(s + 1) * 512], start=True, stop=True)
            return mm
        mm_bc3 = I("te", sMM, bc3_mm)

        W("sc", sMM, mm_bc3)
        sc_bc3 = I("sc", sSC, lambda e: e.copy(bc3a[:], BCc[:]))
        pbc3.close()

        pmm = ExitStack()
        PM = [pmm.enter_context(nc.psum_tensor(f"PM{i}", [128, TQ], F32)) for i in range(2)]
        ST4X = pmm.enter_context(nc.psum_tensor("ST4X", [1, TQ], F32))
        ST4S = pmm.enter_context(nc.psum_tensor("ST4S", [1, TQ], F32))

        mx_ev = [0] * NCH
        mx_sq = [0] * NCH
        mx_stat = [0] * NCH
        mx_stat_fns = [None] * NCH
        for e_ in range(NCH):
            if e_ == 0:
                W("te", sSC, sc_bc3)
                W("te", sVE, ve_m3r)
                W("te", sVE, ve_x2_done)
            if e_ >= 2:
                W("te", sVE, mx_ev[e_ - 2])

            def mx_mm(e, e_=e_, PM=PM):
                for d in range(NCH):
                    F16 = F16a if d < 8 else F16b
                    dl = d % 8
                    for s in range(2):
                        e.matmul(
                            PM[e_ % 2][:, s * 512:(s + 1) * 512],
                            F16[:, dl * D + e_ * 128: dl * D + e_ * 128 + 128],
                            y16[:, d * TQ + s * 512: d * TQ + (s + 1) * 512],
                            start=(d == 0), stop=False)
                for s in range(2):
                    mm = e.matmul(PM[e_ % 2][:, s * 512:(s + 1) * 512],
                                  negfs16[0:1, e_ * 128:(e_ + 1) * 128],
                                  m3r16[0:1, s * 512:(s + 1) * 512],
                                  start=False, stop=True)
                return mm
            mmx = I("te", sMM, mx_mm)
            lag = e_ - 2
            if lag >= 0:
                W("te", sSC, mx_sq[lag])
                mx_stat[lag] = I("te", sMM, mx_stat_fns[lag])

            W("ve", sMM, mmx)
            mx_ev[e_] = I("ve", sVE, lambda e, e_=e_, PM=PM: e.tensor_tensor(
                mixed16[:, e_ * TQ:(e_ + 1) * TQ], PM[e_ % 2][:, 0:TQ], bc3a[:], op=ALU.mult))

            W("sc", sVE, mx_ev[e_])
            EM("sc", lambda e, e_=e_: e.add(mixed16[:, e_ * TQ:(e_ + 1) * TQ],
                                            mixed16[:, e_ * TQ:(e_ + 1) * TQ],
                                            bffn_sb[:, e_:e_ + 1]))
            if e_ >= 2:
                W("sc", sMM, mx_stat[e_ - 2])
            mx_sq[e_] = I("sc", sSC, lambda e, e_=e_: e.square(
                (scrA if e_ % 2 == 0 else scrB)[:, 0:TQ], mixed16[:, e_ * TQ:(e_ + 1) * TQ]))

            def mx_stat_mm(e, e_=e_):
                for s in range(2):
                    e.matmul(ST4X[0:1, s * 512:(s + 1) * 512], ones_c16[:, 0:1],
                             mixed16[:, e_ * TQ + s * 512: e_ * TQ + (s + 1) * 512],
                             start=(e_ == 0), stop=(e_ == NCH - 1))
                    mm = e.matmul(ST4S[0:1, s * 512:(s + 1) * 512], ones_c16[:, 0:1],
                                  (scrA if e_ % 2 == 0 else scrB)[:, s * 512:(s + 1) * 512],
                                  start=(e_ == 0), stop=(e_ == NCH - 1))
                return mm
            mx_stat_fns[e_] = mx_stat_mm
        for lag in (NCH - 2, NCH - 1):
            W("te", sSC, mx_sq[lag])
            mx_stat[lag] = I("te", sMM, mx_stat_fns[lag])
        mm_stats4 = sMM.n

        if dbg_out("d_mixed", [128, NCH * TQ]):
            W("gp", sSC, mx_sq[15])
            I("gp", sGD, lambda e: e.dma_start(out=dbg["d_mixed"], in_=mixed16[:]), 16)

        sc_rstd4 = emit_ln_rows(TQ, mm_stats4, ST4X, ST4S)
        pmm.close()

        W("ve", sSC, sc_rstd4)
        EM("ve", lambda e: e.tensor_copy(ROWS[0:1, 2048:2048 + TQ], ROWS[64:65, 0:TQ]))
        ve_mr4 = I("ve", sVE, lambda e: e.tensor_tensor(
            ROWS[32:33, 0:TQ], ROWS[0:1, 0:TQ], ROWS[0:1, 2048:2048 + TQ], op=ALU.mult))

        pbc4 = ExitStack()
        BCe = pbc4.enter_context(nc.psum_tensor("BCe", [128, TQ], F32))
        BCf = pbc4.enter_context(nc.psum_tensor("BCf", [128, TQ], F32))

        W("te", sVE, ve_mr4)

        def bc4_mm(e):
            for s in range(2):
                e.matmul(BCe[:, s * 512:(s + 1) * 512], ones_rf[64:65, :],
                         ROWS[64:65, s * 512:(s + 1) * 512], start=True, stop=True)
                mm = e.matmul(BCf[:, s * 512:(s + 1) * 512], ones_rf[32:33, :],
                              ROWS[32:33, s * 512:(s + 1) * 512], start=True, stop=True)
            return mm
        mm_bc4 = I("te", sMM, bc4_mm)

        W("sc", sMM, mm_bc4)
        EM("sc", lambda e: e.copy(bc4a[:], BCe[:]))
        sc_bc4 = I("sc", sSC, lambda e: e.copy(bc4b[:], BCf[:]))
        pbc4.close()

        out_ve = [0] * NCH
        store_ct = [0] * NCH
        for e_ in range(NCH):
            if e_ == 0:
                W("ve", sSC, sc_bc4)
            W("ve", sSC, mx_sq[e_])
            if e_ >= 2:
                W("ve", sST, store_ct[e_ - 2])
            EM("ve", lambda e, e_=e_: e.tensor_tensor(
                scrA[:, 1024:1024 + TQ], mixed16[:, e_ * TQ:(e_ + 1) * TQ], bc4a[:], op=ALU.mult))
            EM("ve", lambda e, e_=e_: e.tensor_tensor(
                scrB[:, 1024:1024 + TQ], scrA[:, 1024:1024 + TQ], bc4b[:], op=ALU.subtract))
            out_ve[e_] = I("ve", sVE, lambda e, e_=e_: e.tensor_tensor(
                outb[e_ % 2][:], y16[:, e_ * TQ:(e_ + 1) * TQ],
                scrB[:, 1024:1024 + TQ], op=ALU.add))

            qk = "sy" if e_ % 2 == 0 else "gp"
            W(qk, sVE, out_ve[e_])
            store_ct[e_] = I(qk, sST, lambda e, e_=e_: e.dma_start(
                out=out_t[e_ * 128:(e_ + 1) * 128, :], in_=outb[e_ % 2][:]), 16)

        ff.close()
        ffb.close()
        ffa.close()
        fc_y.close()

        W("sy", sST, store_ct[NCH - 1])
        W("gp", sST, store_ct[NCH - 1])
        W("gp", sGD, sGD.n)
        flush()

    return nc, sorted(dbg.keys())


_BASE = 10000.0


def _tables():
    theta = 1.0 / _BASE ** (np.arange(0, RD, 2, dtype=np.float64) / RD)
    ang = np.arange(L, dtype=np.float64)[:, None] * theta[None, :]
    sin2 = np.repeat(np.sin(ang), 2, axis=1)
    cos2 = np.repeat(np.cos(ang), 2, axis=1)
    tabc = np.concatenate([sin2.T, sin2.T], axis=0).astype(np.float32)
    tabs = np.concatenate([cos2.T, cos2.T], axis=0).astype(np.float32)
    return np.ascontiguousarray(tabc), np.ascontiguousarray(tabs)


def _psgn():
    p = np.zeros((128, 128), np.float32)
    for base in (0, 64):
        for i in range(32):
            p[base + 2 * i + 1, base + 2 * i] = -1.0
            p[base + 2 * i, base + 2 * i + 1] = 1.0
    return p


def _qr_perm():
    idx = []
    for j in range(8):
        idx += list(range((2 * j + 1) * 64, (2 * j + 2) * 64))
        idx += list(range(2 * j * 64, (2 * j + 1) * 64))
    return np.asarray(idx)


def _wfc_e(wfc):
    wt = np.asarray(wfc, dtype=np.float32).T
    a = wt.reshape(8, 128, NCH, 128)
    a = a.transpose(1, 2, 0, 3).reshape(128, NCH * 1024)
    return np.ascontiguousarray(a)


def _pc(b):
    b = np.asarray(b, dtype=np.float32)
    return np.ascontiguousarray(b.reshape(-1, 128).T)


_CACHE = {}


def kernel(**inp):
    f32 = lambda x: np.ascontiguousarray(np.asarray(x, dtype=np.float32))
    h = f32(inp["L0_attn_h"])
    factors = f32(inp["L0_ffn_factors"])

    key = (tuple(sorted(DEBUG)), STOP)
    if key not in _CACHE:
        _CACHE[key] = build_nc(DEBUG, stop=STOP)
    nc, dbg_names = _CACHE[key]

    tabc, tabs = _tables()
    perm = _qr_perm()
    wkrT = f32(np.asarray(inp["L0_attn_W_kr"]).T)
    shared = {
        "wdkvT": f32(np.asarray(inp["L0_attn_W_down_kv"]).T),
        "wdqT": f32(np.asarray(inp["L0_attn_W_down_q"]).T),
        "wkr2T": np.ascontiguousarray(np.concatenate([wkrT, wkrT], axis=1)),
        "wupkT": f32(np.asarray(inp["L0_attn_W_up_k"]).T),
        "wupvT": f32(np.asarray(inp["L0_attn_W_up_v"]).T),
        "wupqT": f32(np.asarray(inp["L0_attn_W_up_q"]).T),
        "wqrTp": f32(np.asarray(inp["L0_attn_W_qr"]).T[:, perm]),
        "wfcTe": _wfc_e(inp["L0_attn_W_fc"]),
        "bdkv": _pc(inp["L0_attn_b_down_kv"]),
        "bdq": _pc(inp["L0_attn_b_down_q"]),
        "bkr2": _pc(np.concatenate([np.asarray(inp["L0_attn_b_kr"]),
                                    np.asarray(inp["L0_attn_b_kr"])])),
        "bupk": _pc(inp["L0_attn_b_up_k"]),
        "bupv": f32(inp["L0_attn_b_up_v"]),
        "bupq": _pc(inp["L0_attn_b_up_q"]),
        "bqrp": _pc(np.asarray(inp["L0_attn_b_qr"])[perm]),
        "bfc": _pc(inp["L0_attn_b_fc"]),
        "bffn": _pc(inp["L0_ffn_bias"]),
        "psgn": _psgn(),
    }

    in_maps = []
    for c in range(8):
        b, qh = c // 2, c % 2
        q0 = qh * TQ
        m = dict(shared)
        m["ht"] = np.ascontiguousarray(np.roll(h[b].T, -q0, axis=1))
        m["ktabc"] = np.ascontiguousarray(np.roll(tabc, -q0, axis=1))
        m["ktabs"] = np.ascontiguousarray(np.roll(tabs, -q0, axis=1))
        m["factc"] = np.ascontiguousarray(factors[c])
        in_maps.append(m)

    res = run_bass_kernel_spmd(nc, in_maps, core_ids=list(range(8)))

    global LAST_DEBUG
    LAST_DEBUG = {}
    if dbg_names:
        LAST_DEBUG = {n: [np.asarray(res.results[c][n]) for c in range(8)]
                      for n in dbg_names}

    out = np.empty((B, L, D), np.float32)
    for c in range(8):
        b, qh = c // 2, c % 2
        q0 = qh * TQ
        if "out_t" in res.results[c]:
            out[b, q0:q0 + TQ, :] = np.asarray(res.results[c]["out_t"]).T
    return out


# version 18
# speedup vs baseline: 1.0580x; 1.0580x over previous
import math
from contextlib import ExitStack

import numpy as np

import concourse.bass as bass
import concourse.mybir as mybir
from concourse.bass_utils import run_bass_kernel_spmd

F32 = mybir.dt.float32
BF16 = mybir.dt.bfloat16
AF = mybir.ActivationFunctionType
ALU = mybir.AluOpType

B, L, D = 4, 2048, 2048
T = 2048
TQ = 1024
NCH = D // 128
NH = 16
KT = T // 128
DOWN, UP, RD = 512, 1024, 64
SCALE = 1.0 / math.sqrt(192.0)
EPS = 1e-5
ISQ8 = 1.0 / math.sqrt(8.0)

DEBUG = ()
STOP = None
LAST_DEBUG = {}


class Sem:
    def __init__(self, es, nc, name):
        self.h = es.enter_context(nc.semaphore(name))
        self.n = 0


def build_nc(debug=(), stop=None):
    debug = set(debug)
    skip_cc = bool(stop) and stop.endswith("0")
    if skip_cc:
        stop = stop[:-1]
    nc = bass.Bass("TRN2", target_bir_lowering=False)

    ht = nc.dram_tensor("ht", [D, T], F32, kind="ExternalInput").ap()
    wdkvT = nc.dram_tensor("wdkvT", [D, DOWN], F32, kind="ExternalInput").ap()
    wdqT = nc.dram_tensor("wdqT", [D, DOWN], F32, kind="ExternalInput").ap()
    wkr2T = nc.dram_tensor("wkr2T", [D, 128], F32, kind="ExternalInput").ap()
    wupkT = nc.dram_tensor("wupkT", [DOWN, UP], F32, kind="ExternalInput").ap()
    wupvT = nc.dram_tensor("wupvT", [DOWN, UP], F32, kind="ExternalInput").ap()
    wupqT = nc.dram_tensor("wupqT", [DOWN, UP], F32, kind="ExternalInput").ap()
    wqrTp = nc.dram_tensor("wqrTp", [DOWN, UP], F32, kind="ExternalInput").ap()
    wfcTe = nc.dram_tensor("wfcTe", [128, NCH * 1024], F32, kind="ExternalInput").ap()
    bdkv = nc.dram_tensor("bdkv", [128, 4], F32, kind="ExternalInput").ap()
    bdq = nc.dram_tensor("bdq", [128, 4], F32, kind="ExternalInput").ap()
    bkr2 = nc.dram_tensor("bkr2", [128, 1], F32, kind="ExternalInput").ap()
    bupk = nc.dram_tensor("bupk", [128, 8], F32, kind="ExternalInput").ap()
    bupv = nc.dram_tensor("bupv", [UP], F32, kind="ExternalInput").ap()
    bupq = nc.dram_tensor("bupq", [128, 8], F32, kind="ExternalInput").ap()
    bqrp = nc.dram_tensor("bqrp", [128, 8], F32, kind="ExternalInput").ap()
    bfc = nc.dram_tensor("bfc", [128, 16], F32, kind="ExternalInput").ap()
    bffn = nc.dram_tensor("bffn", [128, 16], F32, kind="ExternalInput").ap()
    ktabc = nc.dram_tensor("ktabc", [128, T], F32, kind="ExternalInput").ap()
    ktabs = nc.dram_tensor("ktabs", [128, T], F32, kind="ExternalInput").ap()
    psgn = nc.dram_tensor("psgn", [128, 128], F32, kind="ExternalInput").ap()
    factc = nc.dram_tensor("factc", [D, D], F32, kind="ExternalInput").ap()

    out_t = nc.dram_tensor("out_t", [D, TQ], F32, kind="ExternalOutput").ap()

    dbg = {}

    def dbg_out(name, shape, dtype=BF16):
        if name in debug:
            dbg[name] = nc.dram_tensor(name, shape, dtype, kind="ExternalOutput").ap()
            return dbg[name]
        return None

    fact_in = nc.dram_tensor("fact_in", [D, D], F32)
    fact_sum = nc.dram_tensor("fact_sum", [D, D], F32)

    es = ExitStack()
    with es:
        sLB = Sem(es, nc, "sLB")
        sLM = Sem(es, nc, "sLM")
        sLH = Sem(es, nc, "sLH")
        sLT = Sem(es, nc, "sLT")
        sLW = Sem(es, nc, "sLW")
        sLU = Sem(es, nc, "sLU")
        sLF = Sem(es, nc, "sLF")
        sGX = Sem(es, nc, "sGX")
        sGF = Sem(es, nc, "sGF")
        sGD = Sem(es, nc, "sGD")
        sCC = Sem(es, nc, "sCC")
        sMM = Sem(es, nc, "sMM")
        sSC = Sem(es, nc, "sSC")
        sVE = Sem(es, nc, "sVE")
        sST = Sem(es, nc, "sST")

        blk = es.enter_context(nc.Block())

        pend = {"te": [], "sc": [], "ve": [], "sy": [], "gp": []}

        def EM(key, fn):
            pend[key].append(fn)

        def W(key, sem, val):
            if val > 0:
                h = sem.h
                pend[key].append(lambda e: e.wait_ge(h, val))

        def I(key, sem, fn, k=1):
            sem.n += k
            tgt = sem.n
            h = sem.h
            pend[key].append(lambda e: fn(e).then_inc(h, k))
            return tgt

        def flush():
            for key, attr in (("te", "tensor"), ("sc", "scalar"), ("ve", "vector"),
                              ("sy", "sync"), ("gp", "gpsimd")):
                fns = pend[key]
                if not fns:
                    continue
                pend[key] = []
                dec = getattr(blk, attr)

                def _section(e, fns=fns):
                    for f in fns:
                        f(e)
                dec(_section)

        ones_c16 = es.enter_context(nc.sbuf_tensor("ones_c16", [128, 1], BF16))
        ones_r16 = es.enter_context(nc.sbuf_tensor("ones_r16", [1, 128], BF16))
        ones_rf = es.enter_context(nc.sbuf_tensor("ones_rf", [128, 128], F32))
        ROWS = es.enter_context(nc.sbuf_tensor("ROWS", [128, 3072], F32))
        negfs16 = es.enter_context(nc.sbuf_tensor("negfs16", [1, D], BF16))
        m3r16 = es.enter_context(nc.sbuf_tensor("m3r16", [1, TQ], BF16))
        rz16 = es.enter_context(nc.sbuf_tensor("rz16", [1, TQ], BF16))
        ktabc16 = es.enter_context(nc.sbuf_tensor("ktabc16", [128, T], BF16))
        ktabs16 = es.enter_context(nc.sbuf_tensor("ktabs16", [128, T], BF16))
        psgn16 = es.enter_context(nc.sbuf_tensor("psgn16", [128, 128], BF16))
        bdkv_sb = es.enter_context(nc.sbuf_tensor("bdkv_sb", [128, 4], F32))
        bdq_sb = es.enter_context(nc.sbuf_tensor("bdq_sb", [128, 4], F32))
        bkr2_sb = es.enter_context(nc.sbuf_tensor("bkr2_sb", [128, 1], F32))
        bupk_sb = es.enter_context(nc.sbuf_tensor("bupk_sb", [128, 8], F32))
        bupq_sb = es.enter_context(nc.sbuf_tensor("bupq_sb", [128, 8], F32))
        bqrp_sb = es.enter_context(nc.sbuf_tensor("bqrp_sb", [128, 8], F32))
        bfc_sb = es.enter_context(nc.sbuf_tensor("bfc_sb", [128, 16], F32))
        bffn_sb = es.enter_context(nc.sbuf_tensor("bffn_sb", [128, 16], F32))
        bupv_r16 = es.enter_context(nc.sbuf_tensor("bupv_r16", [1, UP], BF16))
        scrA = es.enter_context(nc.sbuf_tensor("scrA", [128, 2048], BF16))
        scrB = es.enter_context(nc.sbuf_tensor("scrB", [128, 2048], BF16))

        st1 = ExitStack()
        ht16 = st1.enter_context(nc.sbuf_tensor("ht16", [128, NCH * T], BF16))
        wdkvT16 = st1.enter_context(nc.sbuf_tensor("wdkvT16", [128, NCH * DOWN], BF16))
        wdqT16 = st1.enter_context(nc.sbuf_tensor("wdqT16", [128, NCH * DOWN], BF16))
        wkr2T16 = st1.enter_context(nc.sbuf_tensor("wkr2T16", [128, NCH * 128], BF16))
        stLN = ExitStack()
        r_bc = stLN.enter_context(nc.sbuf_tensor("r_bc", [128, T], BF16))
        mrow16 = stLN.enter_context(nc.sbuf_tensor("mrow16", [1, T], BF16))
        negw16 = stLN.enter_context(nc.sbuf_tensor("negw16", [1, 2 * DOWN + 128], BF16))
        htf = stLN.enter_context(nc.sbuf_tensor("htf", [128, 2 * T], F32))

        st15 = ExitStack()
        ckv16 = st15.enter_context(nc.sbuf_tensor("ckv16", [128, 4 * T], BF16, side="right"))
        kr_rope16 = st15.enter_context(nc.sbuf_tensor("kr_rope16", [128, T], BF16, side="right"))
        stCQ = ExitStack()
        cq16 = stCQ.enter_context(nc.sbuf_tensor("cq16", [128, 4 * TQ], BF16, side="right"))
        wqr_st = ExitStack()
        wqrT16 = wqr_st.enter_context(nc.sbuf_tensor("wqrT16", [128, 4 * UP], BF16, side="right"))
        kr_st = ExitStack()
        kr_raw16 = kr_st.enter_context(nc.sbuf_tensor("kr_raw16", [128, T], BF16, side="right"))

        for bsrc, bdst in ((bdkv, bdkv_sb), (bdq, bdq_sb), (bupk, bupk_sb),
                           (bupq, bupq_sb), (bqrp, bqrp_sb), (bfc, bfc_sb),
                           (bffn, bffn_sb), (bkr2, bkr2_sb)):
            I("sy", sLM, lambda e, a=bsrc, b=bdst: e.dma_start(out=b[:], in_=a), 16)
        LM_ALL = sLM.n
        ve_base = sVE.n + 3
        conv = [ve_base + k + 1 for k in range(NCH)]
        ht_ld = [0] * NCH
        for k in range(NCH):
            if k >= 2:
                W("sy", sVE, conv[k - 2])
            ht_ld[k] = I("sy", sLH, lambda e, k=k: e.dma_start(
                out=htf[:, (k % 2) * T:(k % 2 + 1) * T],
                in_=ht[k * 128:(k + 1) * 128, :]), 16)
        for w, t, nco in ((wdkvT, wdkvT16, DOWN), (wdqT, wdqT16, DOWN), (wkr2T, wkr2T16, 128)):
            for k in range(NCH):
                I("gp", sLW, lambda e, w=w, t=t, nco=nco, k=k: e.dma_start(
                    out=t[:, k * nco:(k + 1) * nco], in_=w[k * 128:(k + 1) * 128, :]), 16)
        I("gp", sLT, lambda e: e.dma_start(out=ktabc16[:], in_=ktabc), 16)
        I("gp", sLT, lambda e: e.dma_start(out=ktabs16[:], in_=ktabs), 16)
        I("gp", sLT, lambda e: e.dma_start(out=psgn16[:], in_=psgn), 16)
        I("gp", sLT, lambda e: e.dma_start(out=bupv_r16[0:1, :], in_=bupv[None, :]), 16)
        for k in range(4):
            I("gp", sLU, lambda e, k=k: e.dma_start(
                out=wqrT16[:, k * UP:(k + 1) * UP], in_=wqrTp[k * 128:(k + 1) * 128, :]), 16)
        LU_QR = sLU.n
        LT_ALL = sLT.n
        LW_ALL = sLW.n

        EM("ve", lambda e: e.memset(ones_c16[:], 1.0))
        EM("ve", lambda e: e.memset(ones_r16[:], 1.0))
        EM("ve", lambda e: e.memset(ones_rf[:], 1.0))
        sVE.n += 3
        ve_consts = sVE.n
        pend["ve"][-1] = (lambda fn: (lambda e: fn(e).then_inc(sVE.h, 3)))(pend["ve"][-1])
        assert ve_consts == ve_base
        for k in range(NCH):
            W("ve", sLH, ht_ld[k])
            got = I("ve", sVE, lambda e, k=k: e.tensor_copy(
                ht16[:, k * T:(k + 1) * T], htf[:, (k % 2) * T:(k % 2 + 1) * T]))
            assert got == conv[k]
        LH_ALL = sLH.n
        if not skip_cc:
            I("sy", sLB, lambda e: e.dma_start(out=fact_in.ap(), in_=factc), 16)

        p1 = ExitStack()
        ST_X = p1.enter_context(nc.psum_tensor("ST_X", [1, T], F32))
        ST_S = p1.enter_context(nc.psum_tensor("ST_S", [1, T], F32))

        W("sc", sLM, LM_ALL)
        W("te", sVE, ve_consts)
        sq_done = [0] * NCH
        stat_done = [0] * NCH
        for k in range(NCH):
            W("sc", sVE, conv[k])
            if k >= 2:
                W("sc", sMM, stat_done[k - 2])
            sq_done[k] = I("sc", sSC, lambda e, k=k: e.square(
                (scrA if k % 2 == 0 else scrB)[:], ht16[:, k * T:(k + 1) * T]))
            W("te", sSC, sq_done[k])

            def stats_k(e, k=k):
                tgt = scrA if k % 2 == 0 else scrB
                for s in range(4):
                    e.matmul(ST_X[0:1, s * 512:(s + 1) * 512], ones_c16[:, 0:1],
                             ht16[:, k * T + s * 512: k * T + (s + 1) * 512],
                             start=(k == 0), stop=(k == NCH - 1))
                    mm = e.matmul(ST_S[0:1, s * 512:(s + 1) * 512], ones_c16[:, 0:1],
                                  tgt[:, s * 512:(s + 1) * 512],
                                  start=(k == 0), stop=(k == NCH - 1))
                return mm
            stat_done[k] = I("te", sMM, stats_k)
        mm_stats1 = sMM.n

        def emit_ln_rows(n_tok, wait_mm, stx, sts, scr_row=None):
            SCR = scr_row if scr_row is not None else ROWS[0:1, 2048:2048 + n_tok]
            W("sc", sMM, wait_mm)
            sc_rows = I("sc", sSC, lambda e: e.mul(ROWS[0:1, 0:n_tok], stx[0:1, 0:n_tok], 1.0 / D))
            W("ve", sSC, sc_rows)
            EM("ve", lambda e: e.tensor_tensor(SCR, ROWS[0:1, 0:n_tok], ROWS[0:1, 0:n_tok], op=ALU.mult))
            EM("ve", lambda e: e.scalar_tensor_tensor(SCR, sts[0:1, 0:n_tok], 1.0 / D, SCR,
                                                      ALU.mult, ALU.subtract))
            ve_var = I("ve", sVE, lambda e: e.tensor_scalar_add(SCR, SCR, EPS))
            W("sc", sVE, ve_var)
            EM("sc", lambda e: e.activation(ROWS[32:33, 0:n_tok], SCR, AF.Ln))
            return I("sc", sSC, lambda e: e.activation(
                ROWS[64:65, 0:n_tok], ROWS[32:33, 0:n_tok], AF.Exp, scale=-0.5))

        sc_rstd1 = emit_ln_rows(T, mm_stats1, ST_X, ST_S, scr_row=htf[0:1, 0:T])
        p1.close()

        p2 = ExitStack()
        BC2 = p2.enter_context(nc.psum_tensor("BC2", [128, T], F32))
        WS = p2.enter_context(nc.psum_tensor("WS", [1, 2 * DOWN + 128], F32))

        W("te", sSC, sc_rstd1)

        def bc1_mm(e):
            for s in range(4):
                mm = e.matmul(BC2[:, s * 512:(s + 1) * 512], ones_rf[64:65, :],
                              ROWS[64:65, s * 512:(s + 1) * 512], start=True, stop=True)
            return mm
        mm_bc1 = I("te", sMM, bc1_mm)

        def ws_mm(e):
            for k in range(NCH):
                e.matmul(WS[0:1, 0:DOWN], ones_c16[:, 0:1],
                         wdkvT16[:, k * DOWN:(k + 1) * DOWN],
                         start=(k == 0), stop=(k == NCH - 1))
                e.matmul(WS[0:1, DOWN:2 * DOWN], ones_c16[:, 0:1],
                         wdqT16[:, k * DOWN:(k + 1) * DOWN],
                         start=(k == 0), stop=(k == NCH - 1))
                mm = e.matmul(WS[0:1, 2 * DOWN:2 * DOWN + 128], ones_c16[:, 0:1],
                              wkr2T16[:, k * 128:(k + 1) * 128],
                              start=(k == 0), stop=(k == NCH - 1))
            return mm
        W("te", sLW, LW_ALL)
        mm_ws = I("te", sMM, ws_mm)

        W("sc", sMM, mm_bc1)

        def bc1_cp(e):
            for s in range(4):
                e.copy(r_bc[:, s * 512:(s + 1) * 512], BC2[:, s * 512:(s + 1) * 512])
            e.copy(mrow16[0:1, :], ROWS[0:1, 0:T])
            return e.mul(negw16[0:1, :], WS[0:1, :], -1.0)
        W("sc", sMM, mm_ws)
        sc_bc1 = I("sc", sSC, bc1_cp)
        p2.close()
        ve_xs = sVE.n

        def _stop_here(stacks=()):
            W("gp", sGD, sGD.n)
            flush()
            for st in stacks:
                st.close()
            return nc, sorted(dbg.keys())

        if dbg_out("d_xs", [128, T]):
            W("gp", sVE, ve_xs)
            I("gp", sGD, lambda e: e.dma_start(out=dbg["d_xs"], in_=ht16[:, 0:T]), 16)
        if stop == "xs":
            return _stop_here([stLN, st1, kr_st, stCQ, st15])

        p3 = ExitStack()
        PR = [p3.enter_context(nc.psum_tensor(f"PR{i}", [128, T], F32)) for i in range(2)]
        pr_tick = [0]
        pr_ev = []

        def emit_proj(wt16, wt_cols, n_co, n_tok, nw_off, out16, bias_sb, first=None, chunk_gate=None):
            evs = []
            for j in range(n_co):
                tick = pr_tick[0]
                pj = PR[tick % 2]
                if first is not None and j == 0:
                    first()
                if tick >= 2:
                    W("te", sVE, pr_ev[tick - 2])
                for k in range(NCH):
                    if chunk_gate is not None and j == 0:
                        W("ve" if False else "te", sVE, chunk_gate[k])

                    def mmk(e, j=j, pj=pj, wt16=wt16, wt_cols=wt_cols, n_tok=n_tok, k=k):
                        for s in range(n_tok // 512):
                            mm = e.matmul(
                                pj[:, s * 512:(s + 1) * 512],
                                wt16[:, k * wt_cols + j * 128: k * wt_cols + j * 128 + 128],
                                ht16[:, k * T + s * 512: k * T + (s + 1) * 512],
                                start=(k == 0), stop=False)
                        return mm
                    EM("te", mmk)

                def rank1(e, j=j, pj=pj, n_tok=n_tok, nw_off=nw_off):
                    for s in range(n_tok // 512):
                        mm = e.matmul(
                            pj[:, s * 512:(s + 1) * 512],
                            negw16[0:1, nw_off + j * 128: nw_off + j * 128 + 128],
                            mrow16[0:1, s * 512:(s + 1) * 512],
                            start=False, stop=True)
                    return mm
                mmn = I("te", sMM, rank1)
                W("ve", sMM, mmn)
                EM("ve", lambda e, j=j, pj=pj, n_tok=n_tok: e.tensor_tensor(
                    scrA[:, 0:n_tok], pj[:, 0:n_tok], r_bc[:, 0:n_tok], op=ALU.mult))
                ev = I("ve", sVE, lambda e, j=j, n_tok=n_tok: e.tensor_scalar_add(
                    out16[:, j * n_tok:(j + 1) * n_tok], scrA[:, 0:n_tok],
                    bias_sb[:, j:j + 1]))
                pr_ev.append(ev)
                evs.append(ev)
                pr_tick[0] += 1
            return evs

        def _first_ckv():
            W("te", sLW, LW_ALL)
            W("te", sSC, sc_bc1)

        emit_proj(wdkvT16, DOWN, 4, T, 0, ckv16, bdkv_sb,
                  first=_first_ckv, chunk_gate=conv)
        ckv_ev_last = pr_ev[-1]
        emit_proj(wkr2T16, 128, 1, T, 2 * DOWN, kr_raw16, bkr2_sb)
        kr_raw_ev = pr_ev[-1]
        emit_proj(wdqT16, DOWN, 4, TQ, DOWN, cq16, bdq_sb)
        ve_down_done = pr_ev[-1]

        if dbg_out("d_ckv", [128, 4 * T]):
            W("gp", sVE, ckv_ev_last)
            I("gp", sGD, lambda e: e.dma_start(out=dbg["d_ckv"], in_=ckv16[:]), 16)

        kr_tick = pr_tick[0]
        PRk = PR[kr_tick % 2]
        W("te", sVE, kr_raw_ev)
        W("te", sVE, ve_down_done)
        W("te", sLT, LT_ALL)
        W("ve", sLT, LT_ALL)

        def ksw_mm(e):
            for s in range(4):
                mm = e.matmul(PRk[:, s * 512:(s + 1) * 512], psgn16[:, :],
                              kr_raw16[:, s * 512:(s + 1) * 512], start=True, stop=True)
            return mm
        mm_ksw = I("te", sMM, ksw_mm)

        W("ve", sMM, mm_ksw)
        EM("ve", lambda e: e.tensor_tensor(scrA[:], kr_raw16[:], ktabc16[:], op=ALU.mult))
        EM("ve", lambda e: e.tensor_tensor(scrB[:], PRk[:], ktabs16[:], op=ALU.mult))
        ve_krope = I("ve", sVE, lambda e: e.tensor_tensor(kr_rope16[:], scrA[:], scrB[:], op=ALU.add))
        pr_ev.append(ve_krope)
        pr_tick[0] += 1

        if dbg_out("d_krope", [128, T]):
            W("gp", sVE, ve_krope)
            I("gp", sGD, lambda e: e.dma_start(out=dbg["d_krope"], in_=kr_rope16[:]), 16)

        kr_st.close()
        mm_down_done = sMM.n
        stLN.close()
        st1.close()
        if stop == "proj":
            return _stop_here([stCQ, st15])

        ost = ExitStack()
        o_raw = ost.enter_context(nc.sbuf_tensor("o_raw", [128, 8 * TQ], BF16))
        wup = ExitStack()
        wupkT16 = wup.enter_context(nc.sbuf_tensor("wupkT16", [128, 4 * UP], BF16))
        wupvT16 = wup.enter_context(nc.sbuf_tensor("wupvT16", [128, 4 * UP], BF16))
        qst = ExitStack()
        q_all = qst.enter_context(nc.sbuf_tensor("q_all", [128, NH * TQ], BF16))
        qrst = ExitStack()
        qr16 = qrst.enter_context(nc.sbuf_tensor("qr16", [128, 8 * TQ], BF16))
        wupq_st = ExitStack()
        wupqT16 = wupq_st.enter_context(nc.sbuf_tensor("wupqT16", [128, 4 * UP], BF16))

        W("gp", sMM, mm_down_done)
        W("gp", sVE, ve_down_done)
        for w, t in ((wupqT, wupqT16), (wupkT, wupkT16), (wupvT, wupvT16)):
            for k in range(4):
                I("gp", sLU, lambda e, w=w, t=t, k=k: e.dma_start(
                    out=t[:, k * UP:(k + 1) * UP], in_=w[k * 128:(k + 1) * 128, :]), 16)
        LU_ALL = sLU.n

        if not skip_cc:
            W("gp", sLB, 16)
            I("gp", sCC, lambda e: e.collective_compute(
                "AllReduce", ALU.add,
                replica_groups=[list(range(8))],
                ins=[fact_in.ap().opt()],
                outs=[fact_sum.ap().opt()],
            ))

        def emit_up_q(wt16, evac_fn, first=None):
            for j in range(8):
                tick = pr_tick[0]
                pj = PR[tick % 2]
                if first is not None and j == 0:
                    first()
                if tick >= 2:
                    W("te", sVE, pr_ev[tick - 2])

                def mmf(e, j=j, pj=pj, wt16=wt16):
                    for c in range(4):
                        for s in range(2):
                            mm = e.matmul(
                                pj[:, s * 512:(s + 1) * 512],
                                wt16[:, c * UP + j * 128: c * UP + j * 128 + 128],
                                cq16[:, c * TQ + s * 512: c * TQ + (s + 1) * 512],
                                start=(c == 0), stop=(c == 3))
                    return mm
                mmn = I("te", sMM, mmf)
                W("ve", sMM, mmn)
                pr_ev.append(evac_fn(j, pj))
                pr_tick[0] += 1

        def _qr_first():
            W("te", sLU, LU_QR)
            W("te", sVE, ve_down_done)

        emit_up_q(wqrT16,
                  lambda j, pj: I("ve", sVE, lambda e, j=j, pj=pj: e.tensor_scalar_add(
                      qr16[:, j * TQ:(j + 1) * TQ], pj[:, 0:TQ], bqrp_sb[:, j:j + 1])),
                  first=_qr_first)

        def qc_evac(j, pj):
            EM("ve", lambda e, j=j, pj=pj: e.tensor_scalar_add(
                q_all[0:64, (2 * j) * TQ:(2 * j + 1) * TQ],
                pj[0:64, 0:TQ], bupq_sb[0:64, j:j + 1]))
            return I("ve", sVE, lambda e, j=j, pj=pj: e.tensor_scalar_add(
                q_all[64:128, (2 * j + 1) * TQ:(2 * j + 2) * TQ],
                pj[64:128, 0:TQ], bupq_sb[64:128, j:j + 1]))

        emit_up_q(wupqT16, qc_evac,
                  first=lambda: W("te", sLU, LU_QR + 64))

        for j in range(8):
            tick = pr_tick[0]
            pj = PR[tick % 2]
            if tick >= 2:
                W("te", sVE, pr_ev[tick - 2])

            def qsw_mm(e, j=j, pj=pj):
                for s in range(2):
                    mm = e.matmul(pj[:, s * 512:(s + 1) * 512], psgn16[:, :],
                                  qr16[:, j * TQ + s * 512: j * TQ + (s + 1) * 512],
                                  start=True, stop=True)
                return mm
            mm_qsw = I("te", sMM, qsw_mm)

            W("ve", sMM, mm_qsw)
            EM("ve", lambda e, j=j: e.tensor_tensor(
                scrA[:, 0:TQ], qr16[:, j * TQ:(j + 1) * TQ], ktabc16[:, 0:TQ], op=ALU.mult))
            EM("ve", lambda e, pj=pj: e.tensor_tensor(
                scrB[:, 0:TQ], pj[:, 0:TQ], ktabs16[:, 0:TQ], op=ALU.mult))
            EM("ve", lambda e, j=j: e.tensor_tensor(
                q_all[0:64, (2 * j + 1) * TQ:(2 * j + 2) * TQ],
                scrA[0:64, 0:TQ], scrB[0:64, 0:TQ], op=ALU.add))
            qv = I("ve", sVE, lambda e, j=j: e.tensor_tensor(
                q_all[64:128, (2 * j) * TQ:(2 * j + 1) * TQ],
                scrA[64:128, 0:TQ], scrB[64:128, 0:TQ], op=ALU.add))
            pr_ev.append(qv)
            pr_tick[0] += 1
        ve_qall_done = sVE.n

        p3.close()
        wupq_st.close()
        qrst.close()
        wqr_st.close()
        stCQ.close()

        if dbg_out("d_qall", [128, NH * TQ]):
            W("gp", sVE, ve_qall_done)
            I("gp", sGD, lambda e: e.dma_start(out=dbg["d_qall"], in_=q_all[:]), 16)
        if stop == "q":
            return _stop_here([qst, wup, ost, st15])

        katt = ExitStack()
        k_all = katt.enter_context(nc.sbuf_tensor("k_all", [128, 8 * T], BF16))
        v_aug = katt.enter_context(nc.sbuf_tensor("v_aug", [128, 8 * KT * 65], BF16))
        p_buf = [katt.enter_context(nc.sbuf_tensor(f"p_buf{i}", [128, TQ], BF16)) for i in range(4)]
        o_unf = [katt.enter_context(nc.sbuf_tensor(f"o_unf{i}", [128, TQ], F32)) for i in range(2)]

        exp_hist = []
        pv_hist = []
        onorm_hist = []
        group_att_mm = [0, 0]
        group_ve_done = [0, 0]

        for g in range(2):
            pk = ExitStack()
            PK = [pk.enter_context(nc.psum_tensor(f"PK{g}{i}", [128, T], F32)) for i in range(2)]

            if g == 1:
                W("ve", sMM, group_att_mm[0])
                W("ve", sVE, group_ve_done[0])
            ve_vones = I("ve", sVE, lambda e: e.memset(v_aug[:], 1.0))

            kc_ev = [0] * 4
            for j in range(4):
                if g == 0 and j == 0:
                    W("te", sLU, LU_ALL)
                    W("te", sVE, ckv_ev_last)
                    W("te", sVE, ve_qall_done)
                if j >= 2:
                    W("te", sVE, kc_ev[j - 2])
                elif g == 1:
                    W("te", sVE, group_ve_done[0])

                def kc_mm(e, j=j, g=g, PK=PK):
                    for c in range(4):
                        for s in range(4):
                            mm = e.matmul(
                                PK[j % 2][:, s * 512:(s + 1) * 512],
                                wupkT16[:, c * UP + (4 * g + j) * 128: c * UP + (4 * g + j) * 128 + 128],
                                ckv16[:, c * T + s * 512: c * T + (s + 1) * 512],
                                start=(c == 0), stop=(c == 3))
                    return mm
                mmn = I("te", sMM, kc_mm)

                W("ve", sMM, mmn)
                EM("ve", lambda e, j=j, g=g, PK=PK: e.tensor_scalar_add(
                    k_all[0:64, (2 * j) * T:(2 * j + 1) * T],
                    PK[j % 2][0:64, :], bupk_sb[0:64, 4 * g + j:4 * g + j + 1]))
                kc_ev[j] = I("ve", sVE, lambda e, j=j, g=g, PK=PK: e.tensor_scalar_add(
                    k_all[64:128, (2 * j + 1) * T:(2 * j + 2) * T],
                    PK[j % 2][64:128, :], bupk_sb[64:128, 4 * g + j:4 * g + j + 1]))
            kc_done = sVE.n

            W("ve", sVE, ve_krope)
            for hl in range(7):
                if hl % 2 == 0:
                    EM("ve", lambda e, hl=hl: e.tensor_copy(
                        k_all[64:128, hl * T:(hl + 1) * T], kr_rope16[64:128, :]))
                else:
                    EM("ve", lambda e, hl=hl: e.tensor_copy(
                        k_all[0:64, hl * T:(hl + 1) * T], kr_rope16[0:64, :]))
            kr_copy_done = I("ve", sVE, lambda e: e.tensor_copy(
                k_all[0:64, 7 * T:8 * T], kr_rope16[0:64, :]))

            pk.close()
            pv = ExitStack()
            PV1 = [pv.enter_context(nc.psum_tensor(f"PV{g}{i}", [128, 512], F32)) for i in range(2)]
            v_ev = [0] * KT
            for t in range(KT):
                if t == 0:
                    W("te", sVE, kc_done)
                    W("te", sVE, ve_vones)
                if t >= 2:
                    W("te", sVE, v_ev[t - 2])

                def v_mmf(e, t=t, g=g, PV1=PV1):
                    for c in range(4):
                        e.matmul(PV1[t % 2][:, 0:512],
                                 ckv16[:, c * T + t * 128: c * T + t * 128 + 128],
                                 wupvT16[:, c * UP + g * 512: c * UP + (g + 1) * 512],
                                 start=(c == 0), stop=False)
                    return e.matmul(PV1[t % 2][:, 0:512], ones_r16[0:1, :],
                                    bupv_r16[0:1, g * 512:(g + 1) * 512],
                                    start=False, stop=True)
                v_mmn = I("te", sMM, v_mmf)

                W("ve", sMM, v_mmn)
                v_ev[t] = I("ve", sVE, lambda e, t=t, PV1=PV1: e.tensor_copy(
                    v_aug[:].rearrange("p (h c m) -> p h c m", h=8, c=KT)[:, :, t, 0:64],
                    PV1[t % 2][:, 0:512].rearrange("p (h m) -> p h m", h=8)))
            pv.close()
            asm_gate = sVE.n

            if g == 0 and dbg_out("d_kall", [128, 8 * T]):
                W("gp", sVE, asm_gate)
                I("gp", sGD, lambda e: e.dma_start(out=dbg["d_kall"], in_=k_all[:]), 16)
            if g == 0 and dbg_out("d_vaug", [128, 8 * KT * 65]):
                W("gp", sVE, asm_gate)
                I("gp", sGD, lambda e: e.dma_start(out=dbg["d_vaug"], in_=v_aug[:]), 16)

            patt = ExitStack()
            SP = [patt.enter_context(nc.psum_tensor(f"SP{g}{i}", [128, TQ], F32)) for i in range(2)]
            O_ps = patt.enter_context(nc.psum_tensor(f"O{g}", [128, TQ], F32))
            RZ = patt.enter_context(nc.psum_tensor(f"RZ{g}", [64, TQ], F32))

            rz_sc = [None]
            rz_te = [None]
            onorm_tt = [0]

            def emit_rz_sc():
                if rz_sc[0] is None:
                    return
                ou, ouc, dst = rz_sc[0]
                rz_sc[0] = None
                W("sc", sVE, ouc)
                EM("sc", lambda e, ou=ou: e.activation(
                    ROWS[0:1, 2048:2048 + TQ], ou[64:65, :], AF.Ln))
                sc_rz1 = I("sc", sSC, lambda e: e.activation(
                    rz16[0:1, :], ROWS[0:1, 2048:2048 + TQ], AF.Exp, scale=-1.0))
                rz_te[0] = (ou, sc_rz1, dst, onorm_tt[0])

            def emit_rz_te(exp_c4):
                if rz_te[0] is None:
                    return
                ou, sc_rz1, dst, prev_onorm = rz_te[0]
                rz_te[0] = None
                W("te", sSC, sc_rz1)
                if prev_onorm:
                    W("te", sVE, prev_onorm)

                def rz_mm(e, RZ=RZ):
                    for s in range(2):
                        mm = e.matmul(RZ[0:64, s * 512:(s + 1) * 512], ones_r16[0:1, 0:64],
                                      rz16[0:1, s * 512:(s + 1) * 512], start=True, stop=True)
                    return mm
                mm_rz = I("te", sMM, rz_mm)
                W("ve", sMM, mm_rz)
                onorm_tt[0] = I("ve", sVE, lambda e, dst=dst, ou=ou, RZ=RZ: e.tensor_tensor(
                    dst, ou[0:64, :], RZ[0:64, :], op=ALU.mult))

            LEAD = 2
            G0 = 16 * 8 * g

            def pv_emit(i):
                hl_, c = (i - G0) // 16, (i - G0) % 16
                W("te", sSC, exp_hist[i])
                if c == 0 and onorm_hist:
                    W("te", sVE, onorm_hist[-1])

                def f(e, c=c, hl_=hl_):
                    for s in range(2):
                        mm = e.matmul(
                            O_ps[0:65, s * 512:(s + 1) * 512],
                            v_aug[:, (hl_ * KT + c) * 65:(hl_ * KT + c) * 65 + 65],
                            p_buf[c % 4][:, s * 512:(s + 1) * 512],
                            start=(c == 0), stop=(c == KT - 1))
                    return mm
                pv_hist.append(I("te", sMM, f))
                if c == KT - 1:
                    hg = 8 * g + hl_
                    ou = o_unf[hl_ % 2]
                    W("ve", sMM, pv_hist[-1])
                    ouc = I("ve", sVE, lambda e, ou=ou: e.tensor_copy(
                        ou[0:65, :], O_ps[0:65, :]))
                    onorm_hist.append(ouc)
                    jj, par = hg // 2, hg % 2
                    dst = o_raw[0:64, jj * TQ:(jj + 1) * TQ] if par == 0 else \
                        o_raw[64:128, jj * TQ:(jj + 1) * TQ]
                    rz_sc[0] = (ou, ouc, dst)

            for hl in range(8):
                hglob = 8 * g + hl
                for c in range(KT):
                    i = 16 * hglob + c
                    if c == 0 and hl == 0:
                        W("te", sVE, asm_gate)
                        W("te", sVE, kr_copy_done)
                        W("te", sVE, ve_qall_done)
                    if i - G0 >= 2:
                        W("te", sSC, exp_hist[i - 2])

                    def sc_f(e, c=c, hl=hl, hglob=hglob, SP=SP):
                        for s in range(2):
                            mm = e.matmul(
                                SP[c % 2][:, s * 512:(s + 1) * 512],
                                k_all[:, hl * T + c * 128: hl * T + c * 128 + 128],
                                q_all[:, hglob * TQ + s * 512: hglob * TQ + (s + 1) * 512],
                                start=True, stop=True)
                        return mm
                    score_mm = I("te", sMM, sc_f)

                    if i - G0 >= LEAD:
                        pv_emit(i - LEAD)

                    W("sc", sMM, score_mm)
                    if i - G0 >= 4:
                        W("sc", sMM, pv_hist[i - 4])
                    exp_hist.append(I("sc", sSC, lambda e, c=c, SP=SP: e.activation(
                        p_buf[c % 4][:], SP[c % 2][:], AF.Exp, scale=SCALE)))
                    if c == 5:
                        emit_rz_sc()
                    if c == 8:
                        emit_rz_te(exp_hist[16 * hglob + 7])
            for i in range(16 * 8 * (g + 1) - LEAD, 16 * 8 * (g + 1)):
                pv_emit(i)
            emit_rz_sc()
            emit_rz_te(exp_hist[-1])
            group_ve_done[g] = sVE.n
            group_att_mm[g] = sMM.n
            patt.close()

        o_done = sVE.n
        att_free_mm = sMM.n

        if dbg_out("d_o", [128, 8 * TQ]):
            W("gp", sVE, o_done)
            I("gp", sGD, lambda e: e.dma_start(out=dbg["d_o"], in_=o_raw[:]), 16)
        if stop == "attn":
            return _stop_here([katt, qst, wup, ost, st15])

        katt.close()
        qst.close()
        wup.close()
        st15.close()

        fc_w = ExitStack()
        wfcT16 = fc_w.enter_context(nc.sbuf_tensor("wfcT16", [128, NCH * 1024], BF16))
        fc_y = ExitStack()
        y16 = fc_y.enter_context(nc.sbuf_tensor("y16", [128, NCH * TQ], BF16, side="right"))
        ffa = ExitStack()
        F16a = ffa.enter_context(nc.sbuf_tensor("F16a", [128, 8 * D], BF16, side="right"))
        fc_x = ExitStack()
        x16 = fc_x.enter_context(nc.sbuf_tensor("x16", [128, NCH * TQ], BF16, side="right"))
        fc_bc = ExitStack()
        bc2a = fc_bc.enter_context(nc.sbuf_tensor("bc2a", [128, TQ], BF16))
        bc2b = fc_bc.enter_context(nc.sbuf_tensor("bc2b", [128, TQ], BF16))

        W("gp", sMM, att_free_mm)
        W("gp", sVE, o_done)
        if not skip_cc:
            W("gp", sCC, 1)
        wfc_ld = [0] * 4
        for k in range(4):
            wfc_ld[k] = I("gp", sLF, lambda e, k=k: e.dma_start(
                out=wfcT16[:, k * 4096:(k + 1) * 4096],
                in_=wfcTe[:, k * 4096:(k + 1) * 4096]), 16)
        for k in range(NCH):
            I("gp", sGX, lambda e, k=k: e.dma_start(
                out=x16[:, k * TQ:(k + 1) * TQ], in_=ht[k * 128:(k + 1) * 128, 0:TQ]), 16)
        for k in range(8):
            I("gp", sGF, lambda e, k=k: e.dma_start(
                out=F16a[:, k * D:(k + 1) * D],
                in_=fact_sum.ap()[k * 128:(k + 1) * 128, :]), 16)
        GF_A = sGF.n
        GX_ALL = sGX.n

        pfc = ExitStack()
        PF = [pfc.enter_context(nc.psum_tensor(f"PF{i}", [128, TQ], F32)) for i in range(2)]
        ST2X = pfc.enter_context(nc.psum_tensor("ST2X", [1, TQ], F32))
        ST2S = pfc.enter_context(nc.psum_tensor("ST2S", [1, TQ], F32))

        y_ev = [0] * NCH
        y_sq = [0] * NCH
        y_stat = [0] * NCH
        y_stat_fns = [None] * NCH
        for e_ in range(NCH):
            if e_ == 0:
                W("te", sVE, o_done)
            if e_ % 4 == 0:
                W("te", sLF, wfc_ld[e_ // 4])
            if e_ >= 2:
                W("te", sVE, y_ev[e_ - 2])

            def fc_mm(e, e_=e_, PF=PF):
                for c in range(8):
                    for s in range(2):
                        mm = e.matmul(
                            PF[e_ % 2][:, s * 512:(s + 1) * 512],
                            wfcT16[:, e_ * 1024 + c * 128: e_ * 1024 + c * 128 + 128],
                            o_raw[:, c * TQ + s * 512: c * TQ + (s + 1) * 512],
                            start=(c == 0), stop=(c == 7))
                return mm
            mmfc = I("te", sMM, fc_mm)
            lag = e_ - 2
            if lag >= 0:
                W("te", sSC, y_sq[lag])
                y_stat[lag] = I("te", sMM, y_stat_fns[lag])

            W("ve", sMM, mmfc)
            y_ev[e_] = I("ve", sVE, lambda e, e_=e_, PF=PF: e.tensor_scalar_add(
                y16[:, e_ * TQ:(e_ + 1) * TQ], PF[e_ % 2][:, 0:TQ], bfc_sb[:, e_:e_ + 1]))

            W("sc", sVE, y_ev[e_])
            if e_ >= 2:
                W("sc", sMM, y_stat[e_ - 2])
            y_sq[e_] = I("sc", sSC, lambda e, e_=e_: e.square(
                (scrA if e_ % 2 == 0 else scrB)[:, 0:TQ], y16[:, e_ * TQ:(e_ + 1) * TQ]))

            def y_stat_mm(e, e_=e_):
                for s in range(2):
                    e.matmul(ST2X[0:1, s * 512:(s + 1) * 512], ones_c16[:, 0:1],
                             y16[:, e_ * TQ + s * 512: e_ * TQ + (s + 1) * 512],
                             start=(e_ == 0), stop=(e_ == NCH - 1))
                    mm = e.matmul(ST2S[0:1, s * 512:(s + 1) * 512], ones_c16[:, 0:1],
                                  (scrA if e_ % 2 == 0 else scrB)[:, s * 512:(s + 1) * 512],
                                  start=(e_ == 0), stop=(e_ == NCH - 1))
                return mm
            y_stat_fns[e_] = y_stat_mm
        for lag in (NCH - 2, NCH - 1):
            W("te", sSC, y_sq[lag])
            y_stat[lag] = I("te", sMM, y_stat_fns[lag])
        mm_stats2 = sMM.n
        mm_fc_done = mm_stats2

        if dbg_out("d_y", [128, NCH * TQ]):
            W("gp", sVE, y_ev[15])
            I("gp", sGD, lambda e: e.dma_start(out=dbg["d_y"], in_=y16[:]), 16)

        sc_rstd2 = emit_ln_rows(TQ, mm_stats2, ST2X, ST2S)
        pfc.close()

        W("ve", sSC, sc_rstd2)
        EM("ve", lambda e: e.tensor_copy(ROWS[0:1, 2048:2048 + TQ], ROWS[64:65, 0:TQ]))
        ve_mr2 = I("ve", sVE, lambda e: e.tensor_tensor(
            ROWS[32:33, 0:TQ], ROWS[0:1, 0:TQ], ROWS[0:1, 2048:2048 + TQ], op=ALU.mult))

        pbc2 = ExitStack()
        BCa = pbc2.enter_context(nc.psum_tensor("BCa", [128, TQ], F32))
        BCb = pbc2.enter_context(nc.psum_tensor("BCb", [128, TQ], F32))

        W("te", sVE, ve_mr2)

        def bc2_mm(e):
            for s in range(2):
                e.matmul(BCa[:, s * 512:(s + 1) * 512], ones_rf[64:65, :],
                         ROWS[64:65, s * 512:(s + 1) * 512], start=True, stop=True)
                mm = e.matmul(BCb[:, s * 512:(s + 1) * 512], ones_rf[32:33, :],
                              ROWS[32:33, s * 512:(s + 1) * 512], start=True, stop=True)
            return mm
        mm_bc2 = I("te", sMM, bc2_mm)

        W("sc", sMM, mm_bc2)
        EM("sc", lambda e: e.copy(bc2a[:], BCa[:]))
        sc_bc2 = I("sc", sSC, lambda e: e.copy(bc2b[:], BCb[:]))
        pbc2.close()

        x2_ve = [0] * NCH
        for e_ in range(NCH):
            if e_ == 0:
                W("ve", sSC, sc_bc2)
                W("ve", sGX, GX_ALL)
            W("ve", sSC, y_sq[e_])
            EM("ve", lambda e, e_=e_: e.tensor_tensor(
                scrA[:, 1024:1024 + TQ], y16[:, e_ * TQ:(e_ + 1) * TQ], bc2a[:], op=ALU.mult))
            EM("ve", lambda e, e_=e_: e.tensor_tensor(
                scrB[:, 1024:1024 + TQ], scrA[:, 1024:1024 + TQ], bc2b[:], op=ALU.subtract))
            x2_ve[e_] = I("ve", sVE, lambda e, e_=e_: e.tensor_tensor(
                y16[:, e_ * TQ:(e_ + 1) * TQ], x16[:, e_ * TQ:(e_ + 1) * TQ],
                scrB[:, 1024:1024 + TQ], op=ALU.add))
        ve_x2_done = x2_ve[15]

        if dbg_out("d_x2", [128, NCH * TQ]):
            W("gp", sVE, ve_x2_done)
            I("gp", sGD, lambda e: e.dma_start(out=dbg["d_x2"], in_=y16[:]), 16)
        if stop == "x2":
            return _stop_here([fc_bc, fc_x, fc_w, ost, fc_y])

        fc_bc.close()
        fc_x.close()
        fc_w.close()
        ost.close()

        ffb = ExitStack()
        F16b = ffb.enter_context(nc.sbuf_tensor("F16b", [128, 8 * D], BF16))
        ff = ExitStack()
        mixed16 = ff.enter_context(nc.sbuf_tensor("mixed16", [128, NCH * TQ], BF16, side="right"))
        bc3a = ff.enter_context(nc.sbuf_tensor("bc3a", [128, TQ], BF16, side="right"))
        bc4a = ff.enter_context(nc.sbuf_tensor("bc4a", [128, TQ], BF16, side="right"))
        bc4b = ff.enter_context(nc.sbuf_tensor("bc4b", [128, TQ], BF16, side="right"))
        outb = [ff.enter_context(nc.sbuf_tensor(f"outb{i}", [128, TQ], F32, side="right")) for i in range(2)]


        W("gp", sMM, mm_fc_done)
        for k in range(8):
            I("gp", sGF, lambda e, k=k: e.dma_start(
                out=F16b[:, k * D:(k + 1) * D],
                in_=fact_sum.ap()[(8 + k) * 128:(8 + k + 1) * 128, :]), 16)
        GF_ALL = sGF.n

        pff = ExitStack()
        ST3X = pff.enter_context(nc.psum_tensor("ST3X", [1, TQ], F32))
        ST3S = pff.enter_context(nc.psum_tensor("ST3S", [1, TQ], F32))
        ST3F = pff.enter_context(nc.psum_tensor("ST3F", [1, D], F32))

        x2_sq = [0] * NCH
        x2_stat = [0] * NCH
        for e_ in range(NCH):
            W("sc", sVE, x2_ve[e_])
            if e_ >= 2:
                W("sc", sMM, x2_stat[e_ - 2])
            x2_sq[e_] = I("sc", sSC, lambda e, e_=e_: e.square(
                (scrA if e_ % 2 == 0 else scrB)[:, 0:TQ], y16[:, e_ * TQ:(e_ + 1) * TQ]))

            W("te", sSC, x2_sq[e_])

            def x2_stat_mm(e, e_=e_):
                for s in range(2):
                    e.matmul(ST3X[0:1, s * 512:(s + 1) * 512], ones_c16[:, 0:1],
                             y16[:, e_ * TQ + s * 512: e_ * TQ + (s + 1) * 512],
                             start=(e_ == 0), stop=(e_ == NCH - 1))
                    mm = e.matmul(ST3S[0:1, s * 512:(s + 1) * 512], ones_c16[:, 0:1],
                                  (scrA if e_ % 2 == 0 else scrB)[:, s * 512:(s + 1) * 512],
                                  start=(e_ == 0), stop=(e_ == NCH - 1))
                return mm
            x2_stat[e_] = I("te", sMM, x2_stat_mm)
        mm_stats3 = sMM.n

        W("te", sGF, GF_A)

        def fsum_a(e):
            for k in range(8):
                for s in range(4):
                    mm = e.matmul(ST3F[0:1, s * 512:(s + 1) * 512], ones_c16[:, 0:1],
                                  F16a[:, k * D + s * 512: k * D + (s + 1) * 512],
                                  start=(k == 0), stop=False)
            return mm
        I("te", sMM, fsum_a)
        W("te", sGF, GF_ALL)

        def fsum_b(e):
            for k in range(8):
                for s in range(4):
                    mm = e.matmul(ST3F[0:1, s * 512:(s + 1) * 512], ones_c16[:, 0:1],
                                  F16b[:, k * D + s * 512: k * D + (s + 1) * 512],
                                  start=False, stop=(k == 7))
            return mm
        mm_fs = I("te", sMM, fsum_b)

        sc_rstd3 = emit_ln_rows(TQ, mm_stats3, ST3X, ST3S)

        EM("sc", lambda e: e.mul(ROWS[32:33, 0:TQ], ROWS[64:65, 0:TQ], ISQ8))
        W("sc", sMM, mm_fs)
        sc_negfs = I("sc", sSC, lambda e: e.mul(negfs16[0:1, 0:D], ST3F[0:1, 0:D], -1.0))
        pff.close()

        W("ve", sSC, sc_negfs)
        ve_m3r = I("ve", sVE, lambda e: e.tensor_copy(m3r16[0:1, 0:TQ], ROWS[0:1, 0:TQ]))

        pbc3 = ExitStack()
        BCc = pbc3.enter_context(nc.psum_tensor("BCc", [128, TQ], F32))

        W("te", sSC, sc_negfs)

        def bc3_mm(e):
            for s in range(2):
                mm = e.matmul(BCc[:, s * 512:(s + 1) * 512], ones_rf[32:33, :],
                              ROWS[32:33, s * 512:(s + 1) * 512], start=True, stop=True)
            return mm
        mm_bc3 = I("te", sMM, bc3_mm)

        W("sc", sMM, mm_bc3)
        sc_bc3 = I("sc", sSC, lambda e: e.copy(bc3a[:], BCc[:]))
        pbc3.close()

        pmm = ExitStack()
        PM = [pmm.enter_context(nc.psum_tensor(f"PM{i}", [128, TQ], F32)) for i in range(2)]
        ST4X = pmm.enter_context(nc.psum_tensor("ST4X", [1, TQ], F32))
        ST4S = pmm.enter_context(nc.psum_tensor("ST4S", [1, TQ], F32))

        mx_ev = [0] * NCH
        mx_sq = [0] * NCH
        mx_stat = [0] * NCH
        mx_stat_fns = [None] * NCH
        for e_ in range(NCH):
            if e_ == 0:
                W("te", sSC, sc_bc3)
                W("te", sVE, ve_m3r)
                W("te", sVE, ve_x2_done)
            if e_ >= 2:
                W("te", sVE, mx_ev[e_ - 2])

            def mx_mm(e, e_=e_, PM=PM):
                for d in range(NCH):
                    F16 = F16a if d < 8 else F16b
                    dl = d % 8
                    for s in range(2):
                        e.matmul(
                            PM[e_ % 2][:, s * 512:(s + 1) * 512],
                            F16[:, dl * D + e_ * 128: dl * D + e_ * 128 + 128],
                            y16[:, d * TQ + s * 512: d * TQ + (s + 1) * 512],
                            start=(d == 0), stop=False)
                for s in range(2):
                    mm = e.matmul(PM[e_ % 2][:, s * 512:(s + 1) * 512],
                                  negfs16[0:1, e_ * 128:(e_ + 1) * 128],
                                  m3r16[0:1, s * 512:(s + 1) * 512],
                                  start=False, stop=True)
                return mm
            mmx = I("te", sMM, mx_mm)
            lag = e_ - 2
            if lag >= 0:
                W("te", sSC, mx_sq[lag])
                mx_stat[lag] = I("te", sMM, mx_stat_fns[lag])

            W("ve", sMM, mmx)
            mx_ev[e_] = I("ve", sVE, lambda e, e_=e_, PM=PM: e.tensor_tensor(
                mixed16[:, e_ * TQ:(e_ + 1) * TQ], PM[e_ % 2][:, 0:TQ], bc3a[:], op=ALU.mult))

            W("sc", sVE, mx_ev[e_])
            EM("sc", lambda e, e_=e_: e.add(mixed16[:, e_ * TQ:(e_ + 1) * TQ],
                                            mixed16[:, e_ * TQ:(e_ + 1) * TQ],
                                            bffn_sb[:, e_:e_ + 1]))
            if e_ >= 2:
                W("sc", sMM, mx_stat[e_ - 2])
            mx_sq[e_] = I("sc", sSC, lambda e, e_=e_: e.square(
                (scrA if e_ % 2 == 0 else scrB)[:, 0:TQ], mixed16[:, e_ * TQ:(e_ + 1) * TQ]))

            def mx_stat_mm(e, e_=e_):
                for s in range(2):
                    e.matmul(ST4X[0:1, s * 512:(s + 1) * 512], ones_c16[:, 0:1],
                             mixed16[:, e_ * TQ + s * 512: e_ * TQ + (s + 1) * 512],
                             start=(e_ == 0), stop=(e_ == NCH - 1))
                    mm = e.matmul(ST4S[0:1, s * 512:(s + 1) * 512], ones_c16[:, 0:1],
                                  (scrA if e_ % 2 == 0 else scrB)[:, s * 512:(s + 1) * 512],
                                  start=(e_ == 0), stop=(e_ == NCH - 1))
                return mm
            mx_stat_fns[e_] = mx_stat_mm
        for lag in (NCH - 2, NCH - 1):
            W("te", sSC, mx_sq[lag])
            mx_stat[lag] = I("te", sMM, mx_stat_fns[lag])
        mm_stats4 = sMM.n

        if dbg_out("d_mixed", [128, NCH * TQ]):
            W("gp", sSC, mx_sq[15])
            I("gp", sGD, lambda e: e.dma_start(out=dbg["d_mixed"], in_=mixed16[:]), 16)

        sc_rstd4 = emit_ln_rows(TQ, mm_stats4, ST4X, ST4S)
        pmm.close()

        W("ve", sSC, sc_rstd4)
        EM("ve", lambda e: e.tensor_copy(ROWS[0:1, 2048:2048 + TQ], ROWS[64:65, 0:TQ]))
        ve_mr4 = I("ve", sVE, lambda e: e.tensor_tensor(
            ROWS[32:33, 0:TQ], ROWS[0:1, 0:TQ], ROWS[0:1, 2048:2048 + TQ], op=ALU.mult))

        pbc4 = ExitStack()
        BCe = pbc4.enter_context(nc.psum_tensor("BCe", [128, TQ], F32))
        BCf = pbc4.enter_context(nc.psum_tensor("BCf", [128, TQ], F32))

        W("te", sVE, ve_mr4)

        def bc4_mm(e):
            for s in range(2):
                e.matmul(BCe[:, s * 512:(s + 1) * 512], ones_rf[64:65, :],
                         ROWS[64:65, s * 512:(s + 1) * 512], start=True, stop=True)
                mm = e.matmul(BCf[:, s * 512:(s + 1) * 512], ones_rf[32:33, :],
                              ROWS[32:33, s * 512:(s + 1) * 512], start=True, stop=True)
            return mm
        mm_bc4 = I("te", sMM, bc4_mm)

        W("sc", sMM, mm_bc4)
        EM("sc", lambda e: e.copy(bc4a[:], BCe[:]))
        sc_bc4 = I("sc", sSC, lambda e: e.copy(bc4b[:], BCf[:]))
        pbc4.close()

        out_ve = [0] * NCH
        store_ct = [0] * NCH
        for e_ in range(NCH):
            if e_ == 0:
                W("ve", sSC, sc_bc4)
            W("ve", sSC, mx_sq[e_])
            if e_ >= 2:
                W("ve", sST, store_ct[e_ - 2])
            EM("ve", lambda e, e_=e_: e.tensor_tensor(
                scrA[:, 1024:1024 + TQ], mixed16[:, e_ * TQ:(e_ + 1) * TQ], bc4a[:], op=ALU.mult))
            EM("ve", lambda e, e_=e_: e.tensor_tensor(
                scrB[:, 1024:1024 + TQ], scrA[:, 1024:1024 + TQ], bc4b[:], op=ALU.subtract))
            out_ve[e_] = I("ve", sVE, lambda e, e_=e_: e.tensor_tensor(
                outb[e_ % 2][:], y16[:, e_ * TQ:(e_ + 1) * TQ],
                scrB[:, 1024:1024 + TQ], op=ALU.add))

            qk = "sy" if e_ % 2 == 0 else "gp"
            W(qk, sVE, out_ve[e_])
            store_ct[e_] = I(qk, sST, lambda e, e_=e_: e.dma_start(
                out=out_t[e_ * 128:(e_ + 1) * 128, :], in_=outb[e_ % 2][:]), 16)

        ff.close()
        ffb.close()
        ffa.close()
        fc_y.close()

        W("sy", sST, store_ct[NCH - 1])
        W("gp", sST, store_ct[NCH - 1])
        W("gp", sGD, sGD.n)
        flush()

    return nc, sorted(dbg.keys())


_BASE = 10000.0


def _tables():
    theta = 1.0 / _BASE ** (np.arange(0, RD, 2, dtype=np.float64) / RD)
    ang = np.arange(L, dtype=np.float64)[:, None] * theta[None, :]
    sin2 = np.repeat(np.sin(ang), 2, axis=1)
    cos2 = np.repeat(np.cos(ang), 2, axis=1)
    tabc = np.concatenate([sin2.T, sin2.T], axis=0).astype(np.float32)
    tabs = np.concatenate([cos2.T, cos2.T], axis=0).astype(np.float32)
    return np.ascontiguousarray(tabc), np.ascontiguousarray(tabs)


def _psgn():
    p = np.zeros((128, 128), np.float32)
    for base in (0, 64):
        for i in range(32):
            p[base + 2 * i + 1, base + 2 * i] = -1.0
            p[base + 2 * i, base + 2 * i + 1] = 1.0
    return p


def _qr_perm():
    idx = []
    for j in range(8):
        idx += list(range((2 * j + 1) * 64, (2 * j + 2) * 64))
        idx += list(range(2 * j * 64, (2 * j + 1) * 64))
    return np.asarray(idx)


def _wfc_e(wfc):
    wt = np.asarray(wfc, dtype=np.float32).T
    a = wt.reshape(8, 128, NCH, 128)
    a = a.transpose(1, 2, 0, 3).reshape(128, NCH * 1024)
    return np.ascontiguousarray(a)


def _pc(b):
    b = np.asarray(b, dtype=np.float32)
    return np.ascontiguousarray(b.reshape(-1, 128).T)


_CACHE = {}


def kernel(**inp):
    f32 = lambda x: np.ascontiguousarray(np.asarray(x, dtype=np.float32))
    h = f32(inp["L0_attn_h"])
    factors = f32(inp["L0_ffn_factors"])

    key = (tuple(sorted(DEBUG)), STOP)
    if key not in _CACHE:
        _CACHE[key] = build_nc(DEBUG, stop=STOP)
    nc, dbg_names = _CACHE[key]

    tabc, tabs = _tables()
    perm = _qr_perm()
    wkrT = f32(np.asarray(inp["L0_attn_W_kr"]).T)
    shared = {
        "wdkvT": f32(np.asarray(inp["L0_attn_W_down_kv"]).T),
        "wdqT": f32(np.asarray(inp["L0_attn_W_down_q"]).T),
        "wkr2T": np.ascontiguousarray(np.concatenate([wkrT, wkrT], axis=1)),
        "wupkT": f32(np.asarray(inp["L0_attn_W_up_k"]).T),
        "wupvT": f32(np.asarray(inp["L0_attn_W_up_v"]).T),
        "wupqT": f32(np.asarray(inp["L0_attn_W_up_q"]).T),
        "wqrTp": f32(np.asarray(inp["L0_attn_W_qr"]).T[:, perm]),
        "wfcTe": _wfc_e(inp["L0_attn_W_fc"]),
        "bdkv": _pc(inp["L0_attn_b_down_kv"]),
        "bdq": _pc(inp["L0_attn_b_down_q"]),
        "bkr2": _pc(np.concatenate([np.asarray(inp["L0_attn_b_kr"]),
                                    np.asarray(inp["L0_attn_b_kr"])])),
        "bupk": _pc(inp["L0_attn_b_up_k"]),
        "bupv": f32(inp["L0_attn_b_up_v"]),
        "bupq": _pc(inp["L0_attn_b_up_q"]),
        "bqrp": _pc(np.asarray(inp["L0_attn_b_qr"])[perm]),
        "bfc": _pc(inp["L0_attn_b_fc"]),
        "bffn": _pc(inp["L0_ffn_bias"]),
        "psgn": _psgn(),
    }

    in_maps = []
    for c in range(8):
        b, qh = c // 2, c % 2
        q0 = qh * TQ
        m = dict(shared)
        m["ht"] = np.ascontiguousarray(np.roll(h[b].T, -q0, axis=1))
        m["ktabc"] = np.ascontiguousarray(np.roll(tabc, -q0, axis=1))
        m["ktabs"] = np.ascontiguousarray(np.roll(tabs, -q0, axis=1))
        m["factc"] = np.ascontiguousarray(factors[c])
        in_maps.append(m)

    res = run_bass_kernel_spmd(nc, in_maps, core_ids=list(range(8)))

    global LAST_DEBUG
    LAST_DEBUG = {}
    if dbg_names:
        LAST_DEBUG = {n: [np.asarray(res.results[c][n]) for c in range(8)]
                      for n in dbg_names}

    out = np.empty((B, L, D), np.float32)
    for c in range(8):
        b, qh = c // 2, c % 2
        q0 = qh * TQ
        if "out_t" in res.results[c]:
            out[b, q0:q0 + TQ, :] = np.asarray(res.results[c]["out_t"]).T
    return out


# version 20 (final)
# speedup vs baseline: 1.1012x; 1.0409x over previous
import math
from contextlib import ExitStack

import numpy as np

import concourse.bass as bass
import concourse.mybir as mybir
from concourse.bass_utils import run_bass_kernel_spmd

F32 = mybir.dt.float32
BF16 = mybir.dt.bfloat16
AF = mybir.ActivationFunctionType
ALU = mybir.AluOpType

B, L, D = 4, 2048, 2048
T = 2048
TQ = 1024
NCH = D // 128
NH = 16
KT = T // 128
DOWN, UP, RD = 512, 1024, 64
SCALE = 1.0 / math.sqrt(192.0)
EPS = 1e-5
ISQ8 = 1.0 / math.sqrt(8.0)

DEBUG = ()
STOP = None
LAST_DEBUG = {}


class Sem:
    def __init__(self, es, nc, name):
        self.h = es.enter_context(nc.semaphore(name))
        self.n = 0


def build_nc(debug=(), stop=None):
    debug = set(debug)
    skip_cc = bool(stop) and stop.endswith("0")
    if skip_cc:
        stop = stop[:-1]
    nc = bass.Bass("TRN2", target_bir_lowering=False)

    ht = nc.dram_tensor("ht", [D, T], F32, kind="ExternalInput").ap()
    wdkvT = nc.dram_tensor("wdkvT", [D, DOWN], F32, kind="ExternalInput").ap()
    wdqT = nc.dram_tensor("wdqT", [D, DOWN], F32, kind="ExternalInput").ap()
    wkr2T = nc.dram_tensor("wkr2T", [D, 128], F32, kind="ExternalInput").ap()
    wupkT = nc.dram_tensor("wupkT", [DOWN, UP], F32, kind="ExternalInput").ap()
    wupvT = nc.dram_tensor("wupvT", [DOWN, UP], F32, kind="ExternalInput").ap()
    wupqT = nc.dram_tensor("wupqT", [DOWN, UP], F32, kind="ExternalInput").ap()
    wqrTp = nc.dram_tensor("wqrTp", [DOWN, UP], F32, kind="ExternalInput").ap()
    wfcTe = nc.dram_tensor("wfcTe", [128, NCH * 1024], F32, kind="ExternalInput").ap()
    bdkv = nc.dram_tensor("bdkv", [128, 4], F32, kind="ExternalInput").ap()
    bdq = nc.dram_tensor("bdq", [128, 4], F32, kind="ExternalInput").ap()
    bkr2 = nc.dram_tensor("bkr2", [128, 1], F32, kind="ExternalInput").ap()
    bupk = nc.dram_tensor("bupk", [128, 8], F32, kind="ExternalInput").ap()
    bupv = nc.dram_tensor("bupv", [UP], F32, kind="ExternalInput").ap()
    bupq = nc.dram_tensor("bupq", [128, 8], F32, kind="ExternalInput").ap()
    bqrp = nc.dram_tensor("bqrp", [128, 8], F32, kind="ExternalInput").ap()
    bfc = nc.dram_tensor("bfc", [128, 16], F32, kind="ExternalInput").ap()
    bffn = nc.dram_tensor("bffn", [128, 16], F32, kind="ExternalInput").ap()
    ktabc = nc.dram_tensor("ktabc", [128, T], F32, kind="ExternalInput").ap()
    ktabs = nc.dram_tensor("ktabs", [128, T], F32, kind="ExternalInput").ap()
    psgn = nc.dram_tensor("psgn", [128, 128], F32, kind="ExternalInput").ap()
    factc = nc.dram_tensor("factc", [D, D], F32, kind="ExternalInput").ap()

    out_t = nc.dram_tensor("out_t", [D, TQ], F32, kind="ExternalOutput").ap()

    dbg = {}

    def dbg_out(name, shape, dtype=BF16):
        if name in debug:
            dbg[name] = nc.dram_tensor(name, shape, dtype, kind="ExternalOutput").ap()
            return dbg[name]
        return None

    fact_in = nc.dram_tensor("fact_in", [D, D], F32)
    fact_sum = nc.dram_tensor("fact_sum", [D, D], F32)

    es = ExitStack()
    with es:
        sLB = Sem(es, nc, "sLB")
        sLM = Sem(es, nc, "sLM")
        sLH = Sem(es, nc, "sLH")
        sLT = Sem(es, nc, "sLT")
        sLW = Sem(es, nc, "sLW")
        sLU = Sem(es, nc, "sLU")
        sLF = Sem(es, nc, "sLF")
        sGX = Sem(es, nc, "sGX")
        sGF = Sem(es, nc, "sGF")
        sGD = Sem(es, nc, "sGD")
        sCC = Sem(es, nc, "sCC")
        sMM = Sem(es, nc, "sMM")
        sSC = Sem(es, nc, "sSC")
        sVE = Sem(es, nc, "sVE")
        sST = Sem(es, nc, "sST")

        blk = es.enter_context(nc.Block())

        pend = {"te": [], "sc": [], "ve": [], "sy": [], "gp": []}

        def EM(key, fn):
            pend[key].append(fn)

        def W(key, sem, val):
            if val > 0:
                h = sem.h
                pend[key].append(lambda e: e.wait_ge(h, val))

        def I(key, sem, fn, k=1):
            sem.n += k
            tgt = sem.n
            h = sem.h
            pend[key].append(lambda e: fn(e).then_inc(h, k))
            return tgt

        def flush():
            for key, attr in (("te", "tensor"), ("sc", "scalar"), ("ve", "vector"),
                              ("sy", "sync"), ("gp", "gpsimd")):
                fns = pend[key]
                if not fns:
                    continue
                pend[key] = []
                dec = getattr(blk, attr)

                def _section(e, fns=fns):
                    for f in fns:
                        f(e)
                dec(_section)

        ones_c16 = es.enter_context(nc.sbuf_tensor("ones_c16", [128, 1], BF16))
        ones_r16 = es.enter_context(nc.sbuf_tensor("ones_r16", [1, 128], BF16))
        ones_rf = es.enter_context(nc.sbuf_tensor("ones_rf", [128, 128], F32))
        ROWS = es.enter_context(nc.sbuf_tensor("ROWS", [128, 3072], F32))
        negfs16 = es.enter_context(nc.sbuf_tensor("negfs16", [1, D], BF16))
        m3r16 = es.enter_context(nc.sbuf_tensor("m3r16", [1, TQ], BF16))
        rz16 = es.enter_context(nc.sbuf_tensor("rz16", [1, TQ], BF16))
        ktabc16 = es.enter_context(nc.sbuf_tensor("ktabc16", [128, T], BF16))
        ktabs16 = es.enter_context(nc.sbuf_tensor("ktabs16", [128, T], BF16))
        psgn16 = es.enter_context(nc.sbuf_tensor("psgn16", [128, 128], BF16))
        bdkv_sb = es.enter_context(nc.sbuf_tensor("bdkv_sb", [128, 4], F32))
        bdq_sb = es.enter_context(nc.sbuf_tensor("bdq_sb", [128, 4], F32))
        bkr2_sb = es.enter_context(nc.sbuf_tensor("bkr2_sb", [128, 1], F32))
        bupk_sb = es.enter_context(nc.sbuf_tensor("bupk_sb", [128, 8], F32))
        bupq_sb = es.enter_context(nc.sbuf_tensor("bupq_sb", [128, 8], F32))
        bqrp_sb = es.enter_context(nc.sbuf_tensor("bqrp_sb", [128, 8], F32))
        bfc_sb = es.enter_context(nc.sbuf_tensor("bfc_sb", [128, 16], F32))
        bffn_sb = es.enter_context(nc.sbuf_tensor("bffn_sb", [128, 16], F32))
        bupv_r16 = es.enter_context(nc.sbuf_tensor("bupv_r16", [1, UP], BF16))
        scrA = es.enter_context(nc.sbuf_tensor("scrA", [128, 2048], BF16))
        scrB = es.enter_context(nc.sbuf_tensor("scrB", [128, 2048], BF16))

        st1 = ExitStack()
        ht16 = st1.enter_context(nc.sbuf_tensor("ht16", [128, NCH * T], BF16))
        wdkvT16 = st1.enter_context(nc.sbuf_tensor("wdkvT16", [128, NCH * DOWN], BF16))
        wdqT16 = st1.enter_context(nc.sbuf_tensor("wdqT16", [128, NCH * DOWN], BF16))
        wkr2T16 = st1.enter_context(nc.sbuf_tensor("wkr2T16", [128, NCH * 128], BF16))
        stLN = ExitStack()
        r_bc = stLN.enter_context(nc.sbuf_tensor("r_bc", [128, T], BF16))
        mrow16 = stLN.enter_context(nc.sbuf_tensor("mrow16", [1, T], BF16))
        negw16 = stLN.enter_context(nc.sbuf_tensor("negw16", [1, 2 * DOWN + 128], BF16))
        htf = stLN.enter_context(nc.sbuf_tensor("htf", [128, 2 * T], F32))

        st15 = ExitStack()
        ckv16 = st15.enter_context(nc.sbuf_tensor("ckv16", [128, 4 * T], BF16, side="right"))
        kr_rope16 = st15.enter_context(nc.sbuf_tensor("kr_rope16", [128, T], BF16, side="right"))
        stCQ = ExitStack()
        cq16 = stCQ.enter_context(nc.sbuf_tensor("cq16", [128, 4 * TQ], BF16, side="right"))
        wqr_st = ExitStack()
        wqrT16 = wqr_st.enter_context(nc.sbuf_tensor("wqrT16", [128, 4 * UP], BF16, side="right"))
        kr_st = ExitStack()
        kr_raw16 = kr_st.enter_context(nc.sbuf_tensor("kr_raw16", [128, T], BF16, side="right"))

        for bsrc, bdst in ((bdkv, bdkv_sb), (bdq, bdq_sb), (bupk, bupk_sb),
                           (bupq, bupq_sb), (bqrp, bqrp_sb), (bfc, bfc_sb),
                           (bffn, bffn_sb), (bkr2, bkr2_sb)):
            I("sy", sLM, lambda e, a=bsrc, b=bdst: e.dma_start(out=b[:], in_=a), 16)
        LM_ALL = sLM.n
        ve_base = sVE.n + 3
        conv = [ve_base + k + 1 for k in range(NCH)]
        ht_ld = [0] * NCH
        for k in range(NCH):
            if k >= 2:
                W("sy", sVE, conv[k - 2])
            ht_ld[k] = I("sy", sLH, lambda e, k=k: e.dma_start(
                out=htf[:, (k % 2) * T:(k % 2 + 1) * T],
                in_=ht[k * 128:(k + 1) * 128, :]), 16)
        for w, t, nco in ((wdkvT, wdkvT16, DOWN), (wdqT, wdqT16, DOWN), (wkr2T, wkr2T16, 128)):
            for k in range(NCH):
                I("gp", sLW, lambda e, w=w, t=t, nco=nco, k=k: e.dma_start(
                    out=t[:, k * nco:(k + 1) * nco], in_=w[k * 128:(k + 1) * 128, :]), 16)
        I("gp", sLT, lambda e: e.dma_start(out=ktabc16[:], in_=ktabc), 16)
        I("gp", sLT, lambda e: e.dma_start(out=ktabs16[:], in_=ktabs), 16)
        I("gp", sLT, lambda e: e.dma_start(out=psgn16[:], in_=psgn), 16)
        I("gp", sLT, lambda e: e.dma_start(out=bupv_r16[0:1, :], in_=bupv[None, :]), 16)
        for k in range(4):
            I("gp", sLU, lambda e, k=k: e.dma_start(
                out=wqrT16[:, k * UP:(k + 1) * UP], in_=wqrTp[k * 128:(k + 1) * 128, :]), 16)
        LU_QR = sLU.n
        LT_ALL = sLT.n
        LW_ALL = sLW.n

        EM("ve", lambda e: e.memset(ones_c16[:], 1.0))
        EM("ve", lambda e: e.memset(ones_r16[:], 1.0))
        EM("ve", lambda e: e.memset(ones_rf[:], 1.0))
        sVE.n += 3
        ve_consts = sVE.n
        pend["ve"][-1] = (lambda fn: (lambda e: fn(e).then_inc(sVE.h, 3)))(pend["ve"][-1])
        assert ve_consts == ve_base
        for k in range(NCH):
            W("ve", sLH, ht_ld[k])
            got = I("ve", sVE, lambda e, k=k: e.tensor_copy(
                ht16[:, k * T:(k + 1) * T], htf[:, (k % 2) * T:(k % 2 + 1) * T]))
            assert got == conv[k]
        LH_ALL = sLH.n
        if not skip_cc:
            I("sy", sLB, lambda e: e.dma_start(out=fact_in.ap(), in_=factc), 16)

        p1 = ExitStack()
        ST_X = p1.enter_context(nc.psum_tensor("ST_X", [1, T], F32))
        ST_S = p1.enter_context(nc.psum_tensor("ST_S", [1, T], F32))

        W("sc", sLM, LM_ALL)
        W("te", sVE, ve_consts)
        sq_done = [0] * NCH
        stat_done = [0] * NCH
        for k in range(NCH):
            W("sc", sVE, conv[k])
            if k >= 2:
                W("sc", sMM, stat_done[k - 2])
            sq_done[k] = I("sc", sSC, lambda e, k=k: e.square(
                (scrA if k % 2 == 0 else scrB)[:], ht16[:, k * T:(k + 1) * T]))
            W("te", sSC, sq_done[k])

            def stats_k(e, k=k):
                tgt = scrA if k % 2 == 0 else scrB
                for s in range(4):
                    e.matmul(ST_X[0:1, s * 512:(s + 1) * 512], ones_c16[:, 0:1],
                             ht16[:, k * T + s * 512: k * T + (s + 1) * 512],
                             start=(k == 0), stop=(k == NCH - 1))
                    mm = e.matmul(ST_S[0:1, s * 512:(s + 1) * 512], ones_c16[:, 0:1],
                                  tgt[:, s * 512:(s + 1) * 512],
                                  start=(k == 0), stop=(k == NCH - 1))
                return mm
            stat_done[k] = I("te", sMM, stats_k)
        mm_stats1 = sMM.n

        def emit_ln_rows(n_tok, wait_mm, stx, sts, scr_row=None):
            SCR = scr_row if scr_row is not None else ROWS[0:1, 2048:2048 + n_tok]
            W("sc", sMM, wait_mm)
            sc_rows = I("sc", sSC, lambda e: e.mul(ROWS[0:1, 0:n_tok], stx[0:1, 0:n_tok], 1.0 / D))
            W("ve", sSC, sc_rows)
            EM("ve", lambda e: e.tensor_tensor(SCR, ROWS[0:1, 0:n_tok], ROWS[0:1, 0:n_tok], op=ALU.mult))
            EM("ve", lambda e: e.scalar_tensor_tensor(SCR, sts[0:1, 0:n_tok], 1.0 / D, SCR,
                                                      ALU.mult, ALU.subtract))
            ve_var = I("ve", sVE, lambda e: e.tensor_scalar_add(SCR, SCR, EPS))
            W("sc", sVE, ve_var)
            EM("sc", lambda e: e.activation(ROWS[32:33, 0:n_tok], SCR, AF.Ln))
            return I("sc", sSC, lambda e: e.activation(
                ROWS[64:65, 0:n_tok], ROWS[32:33, 0:n_tok], AF.Exp, scale=-0.5))

        sc_rstd1 = emit_ln_rows(T, mm_stats1, ST_X, ST_S, scr_row=htf[0:1, 0:T])
        p1.close()

        p2 = ExitStack()
        BC2 = p2.enter_context(nc.psum_tensor("BC2", [128, T], F32))
        WS = p2.enter_context(nc.psum_tensor("WS", [1, 2 * DOWN + 128], F32))

        W("te", sSC, sc_rstd1)

        def bc1_mm(e):
            for s in range(4):
                mm = e.matmul(BC2[:, s * 512:(s + 1) * 512], ones_rf[64:65, :],
                              ROWS[64:65, s * 512:(s + 1) * 512], start=True, stop=True)
            return mm
        mm_bc1 = I("te", sMM, bc1_mm)

        def ws_mm(e):
            for k in range(NCH):
                e.matmul(WS[0:1, 0:DOWN], ones_c16[:, 0:1],
                         wdkvT16[:, k * DOWN:(k + 1) * DOWN],
                         start=(k == 0), stop=(k == NCH - 1))
                e.matmul(WS[0:1, DOWN:2 * DOWN], ones_c16[:, 0:1],
                         wdqT16[:, k * DOWN:(k + 1) * DOWN],
                         start=(k == 0), stop=(k == NCH - 1))
                mm = e.matmul(WS[0:1, 2 * DOWN:2 * DOWN + 128], ones_c16[:, 0:1],
                              wkr2T16[:, k * 128:(k + 1) * 128],
                              start=(k == 0), stop=(k == NCH - 1))
            return mm
        W("te", sLW, LW_ALL)
        mm_ws = I("te", sMM, ws_mm)

        W("sc", sMM, mm_bc1)

        def bc1_cp(e):
            for s in range(4):
                e.copy(r_bc[:, s * 512:(s + 1) * 512], BC2[:, s * 512:(s + 1) * 512])
            e.copy(mrow16[0:1, :], ROWS[0:1, 0:T])
            return e.mul(negw16[0:1, :], WS[0:1, :], -1.0)
        W("sc", sMM, mm_ws)
        sc_bc1 = I("sc", sSC, bc1_cp)
        p2.close()
        ve_xs = sVE.n

        def _stop_here(stacks=()):
            W("gp", sGD, sGD.n)
            flush()
            for st in stacks:
                st.close()
            return nc, sorted(dbg.keys())

        if dbg_out("d_xs", [128, T]):
            W("gp", sVE, ve_xs)
            I("gp", sGD, lambda e: e.dma_start(out=dbg["d_xs"], in_=ht16[:, 0:T]), 16)
        if stop == "xs":
            return _stop_here([stLN, st1, kr_st, stCQ, st15])

        p3 = ExitStack()
        PR = [p3.enter_context(nc.psum_tensor(f"PR{i}", [128, T], F32)) for i in range(2)]
        pr_tick = [0]
        pr_ev = []

        def emit_proj(wt16, wt_cols, n_co, n_tok, nw_off, out16, bias_sb, first=None, chunk_gate=None):
            evs = []
            for j in range(n_co):
                tick = pr_tick[0]
                pj = PR[tick % 2]
                if first is not None and j == 0:
                    first()
                if tick >= 2:
                    W("te", sVE, pr_ev[tick - 2])
                for k in range(NCH):
                    if chunk_gate is not None and j == 0:
                        W("ve" if False else "te", sVE, chunk_gate[k])

                    def mmk(e, j=j, pj=pj, wt16=wt16, wt_cols=wt_cols, n_tok=n_tok, k=k):
                        for s in range(n_tok // 512):
                            mm = e.matmul(
                                pj[:, s * 512:(s + 1) * 512],
                                wt16[:, k * wt_cols + j * 128: k * wt_cols + j * 128 + 128],
                                ht16[:, k * T + s * 512: k * T + (s + 1) * 512],
                                start=(k == 0), stop=False)
                        return mm
                    EM("te", mmk)

                def rank1(e, j=j, pj=pj, n_tok=n_tok, nw_off=nw_off):
                    for s in range(n_tok // 512):
                        mm = e.matmul(
                            pj[:, s * 512:(s + 1) * 512],
                            negw16[0:1, nw_off + j * 128: nw_off + j * 128 + 128],
                            mrow16[0:1, s * 512:(s + 1) * 512],
                            start=False, stop=True)
                    return mm
                mmn = I("te", sMM, rank1)
                W("ve", sMM, mmn)
                EM("ve", lambda e, j=j, pj=pj, n_tok=n_tok: e.tensor_tensor(
                    scrA[:, 0:n_tok], pj[:, 0:n_tok], r_bc[:, 0:n_tok], op=ALU.mult))
                ev = I("ve", sVE, lambda e, j=j, n_tok=n_tok: e.tensor_scalar_add(
                    out16[:, j * n_tok:(j + 1) * n_tok], scrA[:, 0:n_tok],
                    bias_sb[:, j:j + 1]))
                pr_ev.append(ev)
                evs.append(ev)
                pr_tick[0] += 1
            return evs

        def _first_ckv():
            W("te", sLW, LW_ALL)
            W("te", sSC, sc_bc1)

        emit_proj(wdkvT16, DOWN, 4, T, 0, ckv16, bdkv_sb,
                  first=_first_ckv, chunk_gate=conv)
        ckv_ev_last = pr_ev[-1]
        emit_proj(wkr2T16, 128, 1, T, 2 * DOWN, kr_raw16, bkr2_sb)
        kr_raw_ev = pr_ev[-1]
        emit_proj(wdqT16, DOWN, 4, TQ, DOWN, cq16, bdq_sb)
        ve_down_done = pr_ev[-1]

        if dbg_out("d_ckv", [128, 4 * T]):
            W("gp", sVE, ckv_ev_last)
            I("gp", sGD, lambda e: e.dma_start(out=dbg["d_ckv"], in_=ckv16[:]), 16)

        kr_tick = pr_tick[0]
        PRk = PR[kr_tick % 2]
        W("te", sVE, kr_raw_ev)
        W("te", sVE, ve_down_done)
        W("te", sLT, LT_ALL)
        W("ve", sLT, LT_ALL)

        def ksw_mm(e):
            for s in range(4):
                mm = e.matmul(PRk[:, s * 512:(s + 1) * 512], psgn16[:, :],
                              kr_raw16[:, s * 512:(s + 1) * 512], start=True, stop=True)
            return mm
        mm_ksw = I("te", sMM, ksw_mm)

        W("ve", sMM, mm_ksw)
        EM("ve", lambda e: e.tensor_tensor(scrA[:], kr_raw16[:], ktabc16[:], op=ALU.mult))
        EM("ve", lambda e: e.tensor_tensor(scrB[:], PRk[:], ktabs16[:], op=ALU.mult))
        ve_krope = I("ve", sVE, lambda e: e.tensor_tensor(kr_rope16[:], scrA[:], scrB[:], op=ALU.add))
        pr_ev.append(ve_krope)
        pr_tick[0] += 1

        if dbg_out("d_krope", [128, T]):
            W("gp", sVE, ve_krope)
            I("gp", sGD, lambda e: e.dma_start(out=dbg["d_krope"], in_=kr_rope16[:]), 16)

        kr_st.close()
        mm_down_done = sMM.n
        stLN.close()
        st1.close()
        if stop == "proj":
            return _stop_here([stCQ, st15])

        ost = ExitStack()
        o_raw = ost.enter_context(nc.sbuf_tensor("o_raw", [128, 8 * TQ], BF16))
        wup = ExitStack()
        wupkT16 = wup.enter_context(nc.sbuf_tensor("wupkT16", [128, 4 * UP], BF16))
        wupvT16 = wup.enter_context(nc.sbuf_tensor("wupvT16", [128, 4 * UP], BF16))
        qst = ExitStack()
        q_all = qst.enter_context(nc.sbuf_tensor("q_all", [128, NH * TQ], BF16))
        qrst = ExitStack()
        qr16 = qrst.enter_context(nc.sbuf_tensor("qr16", [128, 8 * TQ], BF16))
        wupq_st = ExitStack()
        wupqT16 = wupq_st.enter_context(nc.sbuf_tensor("wupqT16", [128, 4 * UP], BF16))

        W("gp", sMM, mm_down_done)
        W("gp", sVE, ve_down_done)
        for w, t in ((wupqT, wupqT16), (wupkT, wupkT16), (wupvT, wupvT16)):
            for k in range(4):
                I("gp", sLU, lambda e, w=w, t=t, k=k: e.dma_start(
                    out=t[:, k * UP:(k + 1) * UP], in_=w[k * 128:(k + 1) * 128, :]), 16)
        LU_ALL = sLU.n

        if not skip_cc:
            W("gp", sLB, 16)
            I("gp", sCC, lambda e: e.collective_compute(
                "AllReduce", ALU.add,
                replica_groups=[list(range(8))],
                ins=[fact_in.ap().opt()],
                outs=[fact_sum.ap().opt()],
            ))

        def emit_up_q(wt16, evac_fn, first=None):
            for j in range(8):
                tick = pr_tick[0]
                pj = PR[tick % 2]
                if first is not None and j == 0:
                    first()
                if tick >= 2:
                    W("te", sVE, pr_ev[tick - 2])

                def mmf(e, j=j, pj=pj, wt16=wt16):
                    for c in range(4):
                        for s in range(2):
                            mm = e.matmul(
                                pj[:, s * 512:(s + 1) * 512],
                                wt16[:, c * UP + j * 128: c * UP + j * 128 + 128],
                                cq16[:, c * TQ + s * 512: c * TQ + (s + 1) * 512],
                                start=(c == 0), stop=(c == 3))
                    return mm
                mmn = I("te", sMM, mmf)
                W("ve", sMM, mmn)
                pr_ev.append(evac_fn(j, pj))
                pr_tick[0] += 1

        def _qr_first():
            W("te", sLU, LU_QR)
            W("te", sVE, ve_down_done)

        emit_up_q(wqrT16,
                  lambda j, pj: I("ve", sVE, lambda e, j=j, pj=pj: e.tensor_scalar_add(
                      qr16[:, j * TQ:(j + 1) * TQ], pj[:, 0:TQ], bqrp_sb[:, j:j + 1])),
                  first=_qr_first)

        def qc_evac(j, pj):
            EM("ve", lambda e, j=j, pj=pj: e.tensor_scalar_add(
                q_all[0:64, (2 * j) * TQ:(2 * j + 1) * TQ],
                pj[0:64, 0:TQ], bupq_sb[0:64, j:j + 1]))
            return I("ve", sVE, lambda e, j=j, pj=pj: e.tensor_scalar_add(
                q_all[64:128, (2 * j + 1) * TQ:(2 * j + 2) * TQ],
                pj[64:128, 0:TQ], bupq_sb[64:128, j:j + 1]))

        def emit_qc_one(j):
            tick = pr_tick[0]
            pj = PR[tick % 2]
            if j == 0:
                W("te", sLU, LU_QR + 64)
            if tick >= 2:
                W("te", sVE, pr_ev[tick - 2])

            def mmf(e, j=j, pj=pj):
                for c in range(4):
                    for s in range(2):
                        mm = e.matmul(
                            pj[:, s * 512:(s + 1) * 512],
                            wupqT16[:, c * UP + j * 128: c * UP + j * 128 + 128],
                            cq16[:, c * TQ + s * 512: c * TQ + (s + 1) * 512],
                            start=(c == 0), stop=(c == 3))
                return mm
            mmn = I("te", sMM, mmf)
            W("ve", sMM, mmn)
            pr_ev.append(qc_evac(j, pj))
            pr_tick[0] += 1

        def emit_rope_one(j):
            tick = pr_tick[0]
            pj = PR[tick % 2]
            if tick >= 2:
                W("te", sVE, pr_ev[tick - 2])

            def qsw_mm(e, j=j, pj=pj):
                for s in range(2):
                    mm = e.matmul(pj[:, s * 512:(s + 1) * 512], psgn16[:, :],
                                  qr16[:, j * TQ + s * 512: j * TQ + (s + 1) * 512],
                                  start=True, stop=True)
                return mm
            mm_qsw = I("te", sMM, qsw_mm)

            W("ve", sMM, mm_qsw)
            EM("ve", lambda e, j=j: e.tensor_tensor(
                scrA[:, 0:TQ], qr16[:, j * TQ:(j + 1) * TQ], ktabc16[:, 0:TQ], op=ALU.mult))
            EM("ve", lambda e, pj=pj: e.tensor_tensor(
                scrB[:, 0:TQ], pj[:, 0:TQ], ktabs16[:, 0:TQ], op=ALU.mult))
            EM("ve", lambda e, j=j: e.tensor_tensor(
                q_all[0:64, (2 * j + 1) * TQ:(2 * j + 2) * TQ],
                scrA[0:64, 0:TQ], scrB[0:64, 0:TQ], op=ALU.add))
            qv = I("ve", sVE, lambda e, j=j: e.tensor_tensor(
                q_all[64:128, (2 * j) * TQ:(2 * j + 1) * TQ],
                scrA[64:128, 0:TQ], scrB[64:128, 0:TQ], op=ALU.add))
            pr_ev.append(qv)
            pr_tick[0] += 1

        for j in range(8):
            emit_qc_one(j)
        for j in range(8):
            emit_rope_one(j)
        ve_qall_done = sVE.n

        p3.close()
        wupq_st.close()
        qrst.close()
        wqr_st.close()
        stCQ.close()

        if dbg_out("d_qall", [128, NH * TQ]):
            W("gp", sVE, ve_qall_done)
            I("gp", sGD, lambda e: e.dma_start(out=dbg["d_qall"], in_=q_all[:]), 16)
        if stop == "q":
            return _stop_here([qst, wup, ost, st15])

        katt = ExitStack()
        k_all = katt.enter_context(nc.sbuf_tensor("k_all", [128, 8 * T], BF16))
        v_aug = katt.enter_context(nc.sbuf_tensor("v_aug", [128, 8 * KT * 65], BF16))
        p_buf = [katt.enter_context(nc.sbuf_tensor(f"p_buf{i}", [128, TQ], BF16)) for i in range(4)]
        o_unf = [katt.enter_context(nc.sbuf_tensor(f"o_unf{i}", [128, TQ], F32)) for i in range(2)]

        exp_hist = []
        pv_hist = []
        onorm_hist = []
        group_att_mm = [0, 0]
        group_ve_done = [0, 0]

        for g in range(2):
            pk = ExitStack()
            PK = [pk.enter_context(nc.psum_tensor(f"PK{g}{i}", [128, T], F32)) for i in range(2)]

            if g == 1:
                W("ve", sMM, group_att_mm[0])
                W("ve", sVE, group_ve_done[0])
            ve_vones = I("ve", sVE, lambda e: e.memset(v_aug[:], 1.0))

            kc_ev = [0] * 4
            for j in range(4):
                if g == 0 and j == 0:
                    W("te", sLU, LU_ALL)
                    W("te", sVE, ckv_ev_last)
                    W("te", sVE, ve_qall_done)
                if j >= 2:
                    W("te", sVE, kc_ev[j - 2])
                elif g == 1:
                    W("te", sVE, group_ve_done[0])

                def kc_mm(e, j=j, g=g, PK=PK):
                    for c in range(4):
                        for s in range(4):
                            mm = e.matmul(
                                PK[j % 2][:, s * 512:(s + 1) * 512],
                                wupkT16[:, c * UP + (4 * g + j) * 128: c * UP + (4 * g + j) * 128 + 128],
                                ckv16[:, c * T + s * 512: c * T + (s + 1) * 512],
                                start=(c == 0), stop=(c == 3))
                    return mm
                mmn = I("te", sMM, kc_mm)

                W("ve", sMM, mmn)
                EM("ve", lambda e, j=j, g=g, PK=PK: e.tensor_scalar_add(
                    k_all[0:64, (2 * j) * T:(2 * j + 1) * T],
                    PK[j % 2][0:64, :], bupk_sb[0:64, 4 * g + j:4 * g + j + 1]))
                kc_ev[j] = I("ve", sVE, lambda e, j=j, g=g, PK=PK: e.tensor_scalar_add(
                    k_all[64:128, (2 * j + 1) * T:(2 * j + 2) * T],
                    PK[j % 2][64:128, :], bupk_sb[64:128, 4 * g + j:4 * g + j + 1]))
            kc_done = sVE.n

            W("ve", sVE, ve_krope)
            for hl in range(7):
                if hl % 2 == 0:
                    EM("ve", lambda e, hl=hl: e.tensor_copy(
                        k_all[64:128, hl * T:(hl + 1) * T], kr_rope16[64:128, :]))
                else:
                    EM("ve", lambda e, hl=hl: e.tensor_copy(
                        k_all[0:64, hl * T:(hl + 1) * T], kr_rope16[0:64, :]))
            kr_copy_done = I("ve", sVE, lambda e: e.tensor_copy(
                k_all[0:64, 7 * T:8 * T], kr_rope16[0:64, :]))

            pk.close()
            pv = ExitStack()
            PV1 = [pv.enter_context(nc.psum_tensor(f"PV{g}{i}", [128, 512], F32)) for i in range(2)]
            v_ev = [0] * KT
            for t in range(KT):
                if t == 0:
                    W("te", sVE, kc_done)
                    W("te", sVE, ve_vones)
                if t >= 2:
                    W("te", sVE, v_ev[t - 2])

                def v_mmf(e, t=t, g=g, PV1=PV1):
                    for c in range(4):
                        e.matmul(PV1[t % 2][:, 0:512],
                                 ckv16[:, c * T + t * 128: c * T + t * 128 + 128],
                                 wupvT16[:, c * UP + g * 512: c * UP + (g + 1) * 512],
                                 start=(c == 0), stop=False)
                    return e.matmul(PV1[t % 2][:, 0:512], ones_r16[0:1, :],
                                    bupv_r16[0:1, g * 512:(g + 1) * 512],
                                    start=False, stop=True)
                v_mmn = I("te", sMM, v_mmf)

                W("ve", sMM, v_mmn)
                v_ev[t] = I("ve", sVE, lambda e, t=t, PV1=PV1: e.tensor_copy(
                    v_aug[:].rearrange("p (h c m) -> p h c m", h=8, c=KT)[:, :, t, 0:64],
                    PV1[t % 2][:, 0:512].rearrange("p (h m) -> p h m", h=8)))
            pv.close()
            asm_gate = sVE.n

            if g == 0 and dbg_out("d_kall", [128, 8 * T]):
                W("gp", sVE, asm_gate)
                I("gp", sGD, lambda e: e.dma_start(out=dbg["d_kall"], in_=k_all[:]), 16)
            if g == 0 and dbg_out("d_vaug", [128, 8 * KT * 65]):
                W("gp", sVE, asm_gate)
                I("gp", sGD, lambda e: e.dma_start(out=dbg["d_vaug"], in_=v_aug[:]), 16)

            patt = ExitStack()
            SP = [patt.enter_context(nc.psum_tensor(f"SP{g}{i}", [128, TQ], F32)) for i in range(2)]
            O_ps = patt.enter_context(nc.psum_tensor(f"O{g}", [128, TQ], F32))
            RZ = patt.enter_context(nc.psum_tensor(f"RZ{g}", [64, TQ], F32))

            rz_sc = [None]
            rz_te = [None]
            onorm_tt = [0]

            def emit_rz_sc():
                if rz_sc[0] is None:
                    return
                ou, ouc, dst = rz_sc[0]
                rz_sc[0] = None
                W("sc", sVE, ouc)
                EM("sc", lambda e, ou=ou: e.activation(
                    ROWS[0:1, 2048:2048 + TQ], ou[64:65, :], AF.Ln))
                sc_rz1 = I("sc", sSC, lambda e: e.activation(
                    rz16[0:1, :], ROWS[0:1, 2048:2048 + TQ], AF.Exp, scale=-1.0))
                rz_te[0] = (ou, sc_rz1, dst, onorm_tt[0])

            def emit_rz_te(exp_c4):
                if rz_te[0] is None:
                    return
                ou, sc_rz1, dst, prev_onorm = rz_te[0]
                rz_te[0] = None
                W("te", sSC, sc_rz1)
                if prev_onorm:
                    W("te", sVE, prev_onorm)

                def rz_mm(e, RZ=RZ):
                    for s in range(2):
                        mm = e.matmul(RZ[0:64, s * 512:(s + 1) * 512], ones_r16[0:1, 0:64],
                                      rz16[0:1, s * 512:(s + 1) * 512], start=True, stop=True)
                    return mm
                mm_rz = I("te", sMM, rz_mm)
                W("ve", sMM, mm_rz)
                onorm_tt[0] = I("ve", sVE, lambda e, dst=dst, ou=ou, RZ=RZ: e.tensor_tensor(
                    dst, ou[0:64, :], RZ[0:64, :], op=ALU.mult))

            LEAD = 2
            G0 = 16 * 8 * g

            def pv_emit(i):
                hl_, c = (i - G0) // 16, (i - G0) % 16
                W("te", sSC, exp_hist[i])
                if c == 0 and onorm_hist:
                    W("te", sVE, onorm_hist[-1])

                def f(e, c=c, hl_=hl_):
                    for s in range(2):
                        mm = e.matmul(
                            O_ps[0:65, s * 512:(s + 1) * 512],
                            v_aug[:, (hl_ * KT + c) * 65:(hl_ * KT + c) * 65 + 65],
                            p_buf[c % 4][:, s * 512:(s + 1) * 512],
                            start=(c == 0), stop=(c == KT - 1))
                    return mm
                pv_hist.append(I("te", sMM, f))
                if c == KT - 1:
                    hg = 8 * g + hl_
                    ou = o_unf[hl_ % 2]
                    W("ve", sMM, pv_hist[-1])
                    ouc = I("ve", sVE, lambda e, ou=ou: e.tensor_copy(
                        ou[0:65, :], O_ps[0:65, :]))
                    onorm_hist.append(ouc)
                    jj, par = hg // 2, hg % 2
                    dst = o_raw[0:64, jj * TQ:(jj + 1) * TQ] if par == 0 else \
                        o_raw[64:128, jj * TQ:(jj + 1) * TQ]
                    rz_sc[0] = (ou, ouc, dst)

            for hl in range(8):
                hglob = 8 * g + hl
                for c in range(KT):
                    i = 16 * hglob + c
                    if c == 0 and hl == 0:
                        W("te", sVE, asm_gate)
                        W("te", sVE, kr_copy_done)
                        W("te", sVE, ve_qall_done)
                    if i - G0 >= 2:
                        W("te", sSC, exp_hist[i - 2])

                    def sc_f(e, c=c, hl=hl, hglob=hglob, SP=SP):
                        for s in range(2):
                            mm = e.matmul(
                                SP[c % 2][:, s * 512:(s + 1) * 512],
                                k_all[:, hl * T + c * 128: hl * T + c * 128 + 128],
                                q_all[:, hglob * TQ + s * 512: hglob * TQ + (s + 1) * 512],
                                start=True, stop=True)
                        return mm
                    score_mm = I("te", sMM, sc_f)

                    if i - G0 >= LEAD:
                        pv_emit(i - LEAD)

                    W("sc", sMM, score_mm)
                    if i - G0 >= 4:
                        W("sc", sMM, pv_hist[i - 4])
                    exp_hist.append(I("sc", sSC, lambda e, c=c, SP=SP: e.activation(
                        p_buf[c % 4][:], SP[c % 2][:], AF.Exp, scale=SCALE)))
                    if c == 5:
                        emit_rz_sc()
                    if c == 8:
                        emit_rz_te(exp_hist[16 * hglob + 7])
            for i in range(16 * 8 * (g + 1) - LEAD, 16 * 8 * (g + 1)):
                pv_emit(i)
            emit_rz_sc()
            emit_rz_te(exp_hist[-1])
            group_ve_done[g] = sVE.n
            group_att_mm[g] = sMM.n
            patt.close()

        o_done = sVE.n
        att_free_mm = sMM.n

        if dbg_out("d_o", [128, 8 * TQ]):
            W("gp", sVE, o_done)
            I("gp", sGD, lambda e: e.dma_start(out=dbg["d_o"], in_=o_raw[:]), 16)
        if stop == "attn":
            return _stop_here([katt, qst, wup, ost, st15])

        katt.close()
        qst.close()
        wup.close()
        st15.close()

        fc_w = ExitStack()
        wfcT16 = fc_w.enter_context(nc.sbuf_tensor("wfcT16", [128, NCH * 1024], BF16))
        fc_y = ExitStack()
        y16 = fc_y.enter_context(nc.sbuf_tensor("y16", [128, NCH * TQ], BF16, side="right"))
        ffa = ExitStack()
        F16a = ffa.enter_context(nc.sbuf_tensor("F16a", [128, 8 * D], BF16, side="right"))
        fc_x = ExitStack()
        x16 = fc_x.enter_context(nc.sbuf_tensor("x16", [128, NCH * TQ], BF16, side="right"))
        fc_bc = ExitStack()
        bc2a = fc_bc.enter_context(nc.sbuf_tensor("bc2a", [128, TQ], BF16))
        bc2b = fc_bc.enter_context(nc.sbuf_tensor("bc2b", [128, TQ], BF16))

        W("gp", sMM, att_free_mm)
        W("gp", sVE, o_done)
        if not skip_cc:
            W("gp", sCC, 1)
        wfc_ld = [0] * 4
        for k in range(4):
            wfc_ld[k] = I("gp", sLF, lambda e, k=k: e.dma_start(
                out=wfcT16[:, k * 4096:(k + 1) * 4096],
                in_=wfcTe[:, k * 4096:(k + 1) * 4096]), 16)
        for k in range(NCH):
            I("gp", sGX, lambda e, k=k: e.dma_start(
                out=x16[:, k * TQ:(k + 1) * TQ], in_=ht[k * 128:(k + 1) * 128, 0:TQ]), 16)
        for k in range(8):
            I("gp", sGF, lambda e, k=k: e.dma_start(
                out=F16a[:, k * D:(k + 1) * D],
                in_=fact_sum.ap()[k * 128:(k + 1) * 128, :]), 16)
        GF_A = sGF.n
        GX_ALL = sGX.n

        pfc = ExitStack()
        PF = [pfc.enter_context(nc.psum_tensor(f"PF{i}", [128, TQ], F32)) for i in range(2)]
        ST2X = pfc.enter_context(nc.psum_tensor("ST2X", [1, TQ], F32))
        ST2S = pfc.enter_context(nc.psum_tensor("ST2S", [1, TQ], F32))

        y_ev = [0] * NCH
        y_sq = [0] * NCH
        y_stat = [0] * NCH
        y_stat_fns = [None] * NCH
        for e_ in range(NCH):
            if e_ == 0:
                W("te", sVE, o_done)
            if e_ % 4 == 0:
                W("te", sLF, wfc_ld[e_ // 4])
            if e_ >= 2:
                W("te", sVE, y_ev[e_ - 2])

            def fc_mm(e, e_=e_, PF=PF):
                for c in range(8):
                    for s in range(2):
                        mm = e.matmul(
                            PF[e_ % 2][:, s * 512:(s + 1) * 512],
                            wfcT16[:, e_ * 1024 + c * 128: e_ * 1024 + c * 128 + 128],
                            o_raw[:, c * TQ + s * 512: c * TQ + (s + 1) * 512],
                            start=(c == 0), stop=(c == 7))
                return mm
            mmfc = I("te", sMM, fc_mm)
            lag = e_ - 2
            if lag >= 0:
                W("te", sSC, y_sq[lag])
                y_stat[lag] = I("te", sMM, y_stat_fns[lag])

            W("ve", sMM, mmfc)
            y_ev[e_] = I("ve", sVE, lambda e, e_=e_, PF=PF: e.tensor_scalar_add(
                y16[:, e_ * TQ:(e_ + 1) * TQ], PF[e_ % 2][:, 0:TQ], bfc_sb[:, e_:e_ + 1]))

            W("sc", sVE, y_ev[e_])
            if e_ >= 2:
                W("sc", sMM, y_stat[e_ - 2])
            y_sq[e_] = I("sc", sSC, lambda e, e_=e_: e.square(
                (scrA if e_ % 2 == 0 else scrB)[:, 0:TQ], y16[:, e_ * TQ:(e_ + 1) * TQ]))

            def y_stat_mm(e, e_=e_):
                for s in range(2):
                    e.matmul(ST2X[0:1, s * 512:(s + 1) * 512], ones_c16[:, 0:1],
                             y16[:, e_ * TQ + s * 512: e_ * TQ + (s + 1) * 512],
                             start=(e_ == 0), stop=(e_ == NCH - 1))
                    mm = e.matmul(ST2S[0:1, s * 512:(s + 1) * 512], ones_c16[:, 0:1],
                                  (scrA if e_ % 2 == 0 else scrB)[:, s * 512:(s + 1) * 512],
                                  start=(e_ == 0), stop=(e_ == NCH - 1))
                return mm
            y_stat_fns[e_] = y_stat_mm
        for lag in (NCH - 2, NCH - 1):
            W("te", sSC, y_sq[lag])
            y_stat[lag] = I("te", sMM, y_stat_fns[lag])
        mm_stats2 = sMM.n
        mm_fc_done = mm_stats2

        if dbg_out("d_y", [128, NCH * TQ]):
            W("gp", sVE, y_ev[15])
            I("gp", sGD, lambda e: e.dma_start(out=dbg["d_y"], in_=y16[:]), 16)

        sc_rstd2 = emit_ln_rows(TQ, mm_stats2, ST2X, ST2S)
        pfc.close()

        W("ve", sSC, sc_rstd2)
        EM("ve", lambda e: e.tensor_copy(ROWS[0:1, 2048:2048 + TQ], ROWS[64:65, 0:TQ]))
        ve_mr2 = I("ve", sVE, lambda e: e.tensor_tensor(
            ROWS[32:33, 0:TQ], ROWS[0:1, 0:TQ], ROWS[0:1, 2048:2048 + TQ], op=ALU.mult))

        pbc2 = ExitStack()
        BCa = pbc2.enter_context(nc.psum_tensor("BCa", [128, TQ], F32))
        BCb = pbc2.enter_context(nc.psum_tensor("BCb", [128, TQ], F32))

        W("te", sVE, ve_mr2)

        def bc2_mm(e):
            for s in range(2):
                e.matmul(BCa[:, s * 512:(s + 1) * 512], ones_rf[64:65, :],
                         ROWS[64:65, s * 512:(s + 1) * 512], start=True, stop=True)
                mm = e.matmul(BCb[:, s * 512:(s + 1) * 512], ones_rf[32:33, :],
                              ROWS[32:33, s * 512:(s + 1) * 512], start=True, stop=True)
            return mm
        mm_bc2 = I("te", sMM, bc2_mm)

        W("sc", sMM, mm_bc2)
        EM("sc", lambda e: e.copy(bc2a[:], BCa[:]))
        sc_bc2 = I("sc", sSC, lambda e: e.copy(bc2b[:], BCb[:]))
        pbc2.close()

        x2_ve = [0] * NCH
        for e_ in range(NCH):
            if e_ == 0:
                W("ve", sSC, sc_bc2)
                W("ve", sGX, GX_ALL)
            W("ve", sSC, y_sq[e_])
            EM("ve", lambda e, e_=e_: e.tensor_tensor(
                scrA[:, 1024:1024 + TQ], y16[:, e_ * TQ:(e_ + 1) * TQ], bc2a[:], op=ALU.mult))
            EM("ve", lambda e, e_=e_: e.tensor_tensor(
                scrB[:, 1024:1024 + TQ], scrA[:, 1024:1024 + TQ], bc2b[:], op=ALU.subtract))
            x2_ve[e_] = I("ve", sVE, lambda e, e_=e_: e.tensor_tensor(
                y16[:, e_ * TQ:(e_ + 1) * TQ], x16[:, e_ * TQ:(e_ + 1) * TQ],
                scrB[:, 1024:1024 + TQ], op=ALU.add))
        ve_x2_done = x2_ve[15]

        if dbg_out("d_x2", [128, NCH * TQ]):
            W("gp", sVE, ve_x2_done)
            I("gp", sGD, lambda e: e.dma_start(out=dbg["d_x2"], in_=y16[:]), 16)
        if stop == "x2":
            return _stop_here([fc_bc, fc_x, fc_w, ost, fc_y])

        fc_bc.close()
        fc_x.close()
        fc_w.close()
        ost.close()

        ffb = ExitStack()
        F16b = ffb.enter_context(nc.sbuf_tensor("F16b", [128, 8 * D], BF16))
        ff = ExitStack()
        mixed16 = ff.enter_context(nc.sbuf_tensor("mixed16", [128, NCH * TQ], BF16, side="right"))
        bc3a = ff.enter_context(nc.sbuf_tensor("bc3a", [128, TQ], BF16, side="right"))
        bc4a = ff.enter_context(nc.sbuf_tensor("bc4a", [128, TQ], BF16, side="right"))
        bc4b = ff.enter_context(nc.sbuf_tensor("bc4b", [128, TQ], BF16, side="right"))
        outb = [ff.enter_context(nc.sbuf_tensor(f"outb{i}", [128, TQ], F32, side="right")) for i in range(2)]


        W("gp", sMM, mm_fc_done)
        for k in range(8):
            I("gp", sGF, lambda e, k=k: e.dma_start(
                out=F16b[:, k * D:(k + 1) * D],
                in_=fact_sum.ap()[(8 + k) * 128:(8 + k + 1) * 128, :]), 16)
        GF_ALL = sGF.n

        pff = ExitStack()
        ST3X = pff.enter_context(nc.psum_tensor("ST3X", [1, TQ], F32))
        ST3S = pff.enter_context(nc.psum_tensor("ST3S", [1, TQ], F32))
        ST3F = pff.enter_context(nc.psum_tensor("ST3F", [1, D], F32))

        W("te", sGF, GF_A)

        def fsum_a(e):
            for k in range(8):
                for s in range(4):
                    mm = e.matmul(ST3F[0:1, s * 512:(s + 1) * 512], ones_c16[:, 0:1],
                                  F16a[:, k * D + s * 512: k * D + (s + 1) * 512],
                                  start=(k == 0), stop=False)
            return mm
        I("te", sMM, fsum_a)
        W("te", sGF, GF_ALL)

        def fsum_b(e):
            for k in range(8):
                for s in range(4):
                    mm = e.matmul(ST3F[0:1, s * 512:(s + 1) * 512], ones_c16[:, 0:1],
                                  F16b[:, k * D + s * 512: k * D + (s + 1) * 512],
                                  start=False, stop=(k == 7))
            return mm
        mm_fs = I("te", sMM, fsum_b)

        x2_sq = [0] * NCH
        x2_stat = [0] * NCH
        for e_ in range(NCH):
            W("sc", sVE, x2_ve[e_])
            if e_ >= 2:
                W("sc", sMM, x2_stat[e_ - 2])
            x2_sq[e_] = I("sc", sSC, lambda e, e_=e_: e.square(
                (scrA if e_ % 2 == 0 else scrB)[:, 0:TQ], y16[:, e_ * TQ:(e_ + 1) * TQ]))

            W("te", sSC, x2_sq[e_])

            def x2_stat_mm(e, e_=e_):
                for s in range(2):
                    e.matmul(ST3X[0:1, s * 512:(s + 1) * 512], ones_c16[:, 0:1],
                             y16[:, e_ * TQ + s * 512: e_ * TQ + (s + 1) * 512],
                             start=(e_ == 0), stop=(e_ == NCH - 1))
                    mm = e.matmul(ST3S[0:1, s * 512:(s + 1) * 512], ones_c16[:, 0:1],
                                  (scrA if e_ % 2 == 0 else scrB)[:, s * 512:(s + 1) * 512],
                                  start=(e_ == 0), stop=(e_ == NCH - 1))
                return mm
            x2_stat[e_] = I("te", sMM, x2_stat_mm)
        mm_stats3 = sMM.n

        sc_rstd3 = emit_ln_rows(TQ, mm_stats3, ST3X, ST3S)

        EM("sc", lambda e: e.mul(ROWS[32:33, 0:TQ], ROWS[64:65, 0:TQ], ISQ8))
        W("sc", sMM, mm_fs)
        sc_negfs = I("sc", sSC, lambda e: e.mul(negfs16[0:1, 0:D], ST3F[0:1, 0:D], -1.0))
        pff.close()

        W("ve", sSC, sc_negfs)
        ve_m3r = I("ve", sVE, lambda e: e.tensor_copy(m3r16[0:1, 0:TQ], ROWS[0:1, 0:TQ]))

        pbc3 = ExitStack()
        BCc = pbc3.enter_context(nc.psum_tensor("BCc", [128, TQ], F32))

        W("te", sSC, sc_negfs)

        def bc3_mm(e):
            for s in range(2):
                mm = e.matmul(BCc[:, s * 512:(s + 1) * 512], ones_rf[32:33, :],
                              ROWS[32:33, s * 512:(s + 1) * 512], start=True, stop=True)
            return mm
        mm_bc3 = I("te", sMM, bc3_mm)

        W("sc", sMM, mm_bc3)
        sc_bc3 = I("sc", sSC, lambda e: e.copy(bc3a[:], BCc[:]))
        pbc3.close()

        pmm = ExitStack()
        PM = [pmm.enter_context(nc.psum_tensor(f"PM{i}", [128, TQ], F32)) for i in range(2)]
        ST4X = pmm.enter_context(nc.psum_tensor("ST4X", [1, TQ], F32))
        ST4S = pmm.enter_context(nc.psum_tensor("ST4S", [1, TQ], F32))

        mx_ev = [0] * NCH
        mx_sq = [0] * NCH
        mx_stat = [0] * NCH
        mx_stat_fns = [None] * NCH
        for e_ in range(NCH):
            if e_ == 0:
                W("te", sSC, sc_bc3)
                W("te", sVE, ve_m3r)
                W("te", sVE, ve_x2_done)
            if e_ >= 2:
                W("te", sVE, mx_ev[e_ - 2])

            def mx_mm(e, e_=e_, PM=PM):
                for d in range(NCH):
                    F16 = F16a if d < 8 else F16b
                    dl = d % 8
                    for s in range(2):
                        e.matmul(
                            PM[e_ % 2][:, s * 512:(s + 1) * 512],
                            F16[:, dl * D + e_ * 128: dl * D + e_ * 128 + 128],
                            y16[:, d * TQ + s * 512: d * TQ + (s + 1) * 512],
                            start=(d == 0), stop=False)
                for s in range(2):
                    mm = e.matmul(PM[e_ % 2][:, s * 512:(s + 1) * 512],
                                  negfs16[0:1, e_ * 128:(e_ + 1) * 128],
                                  m3r16[0:1, s * 512:(s + 1) * 512],
                                  start=False, stop=True)
                return mm
            mmx = I("te", sMM, mx_mm)
            lag = e_ - 2
            if lag >= 0:
                W("te", sSC, mx_sq[lag])
                mx_stat[lag] = I("te", sMM, mx_stat_fns[lag])

            W("ve", sMM, mmx)
            mx_ev[e_] = I("ve", sVE, lambda e, e_=e_, PM=PM: e.tensor_tensor(
                mixed16[:, e_ * TQ:(e_ + 1) * TQ], PM[e_ % 2][:, 0:TQ], bc3a[:], op=ALU.mult))

            W("sc", sVE, mx_ev[e_])
            EM("sc", lambda e, e_=e_: e.add(mixed16[:, e_ * TQ:(e_ + 1) * TQ],
                                            mixed16[:, e_ * TQ:(e_ + 1) * TQ],
                                            bffn_sb[:, e_:e_ + 1]))
            if e_ >= 2:
                W("sc", sMM, mx_stat[e_ - 2])
            mx_sq[e_] = I("sc", sSC, lambda e, e_=e_: e.square(
                (scrA if e_ % 2 == 0 else scrB)[:, 0:TQ], mixed16[:, e_ * TQ:(e_ + 1) * TQ]))

            def mx_stat_mm(e, e_=e_):
                for s in range(2):
                    e.matmul(ST4X[0:1, s * 512:(s + 1) * 512], ones_c16[:, 0:1],
                             mixed16[:, e_ * TQ + s * 512: e_ * TQ + (s + 1) * 512],
                             start=(e_ == 0), stop=(e_ == NCH - 1))
                    mm = e.matmul(ST4S[0:1, s * 512:(s + 1) * 512], ones_c16[:, 0:1],
                                  (scrA if e_ % 2 == 0 else scrB)[:, s * 512:(s + 1) * 512],
                                  start=(e_ == 0), stop=(e_ == NCH - 1))
                return mm
            mx_stat_fns[e_] = mx_stat_mm
        for lag in (NCH - 2, NCH - 1):
            W("te", sSC, mx_sq[lag])
            mx_stat[lag] = I("te", sMM, mx_stat_fns[lag])
        mm_stats4 = sMM.n

        if dbg_out("d_mixed", [128, NCH * TQ]):
            W("gp", sSC, mx_sq[15])
            I("gp", sGD, lambda e: e.dma_start(out=dbg["d_mixed"], in_=mixed16[:]), 16)

        sc_rstd4 = emit_ln_rows(TQ, mm_stats4, ST4X, ST4S)
        pmm.close()

        W("ve", sSC, sc_rstd4)
        EM("ve", lambda e: e.tensor_copy(ROWS[0:1, 2048:2048 + TQ], ROWS[64:65, 0:TQ]))
        ve_mr4 = I("ve", sVE, lambda e: e.tensor_tensor(
            ROWS[32:33, 0:TQ], ROWS[0:1, 0:TQ], ROWS[0:1, 2048:2048 + TQ], op=ALU.mult))

        pbc4 = ExitStack()
        BCe = pbc4.enter_context(nc.psum_tensor("BCe", [128, TQ], F32))
        BCf = pbc4.enter_context(nc.psum_tensor("BCf", [128, TQ], F32))

        W("te", sVE, ve_mr4)

        def bc4_mm(e):
            for s in range(2):
                e.matmul(BCe[:, s * 512:(s + 1) * 512], ones_rf[64:65, :],
                         ROWS[64:65, s * 512:(s + 1) * 512], start=True, stop=True)
                mm = e.matmul(BCf[:, s * 512:(s + 1) * 512], ones_rf[32:33, :],
                              ROWS[32:33, s * 512:(s + 1) * 512], start=True, stop=True)
            return mm
        mm_bc4 = I("te", sMM, bc4_mm)

        W("sc", sMM, mm_bc4)
        EM("sc", lambda e: e.copy(bc4a[:], BCe[:]))
        sc_bc4 = I("sc", sSC, lambda e: e.copy(bc4b[:], BCf[:]))
        pbc4.close()

        out_ve = [0] * NCH
        store_ct = [0] * NCH
        for e_ in range(NCH):
            if e_ == 0:
                W("ve", sSC, sc_bc4)
            W("ve", sSC, mx_sq[e_])
            if e_ >= 2:
                W("ve", sST, store_ct[e_ - 2])
            EM("ve", lambda e, e_=e_: e.tensor_tensor(
                scrA[:, 1024:1024 + TQ], mixed16[:, e_ * TQ:(e_ + 1) * TQ], bc4a[:], op=ALU.mult))
            EM("ve", lambda e, e_=e_: e.tensor_tensor(
                scrB[:, 1024:1024 + TQ], scrA[:, 1024:1024 + TQ], bc4b[:], op=ALU.subtract))
            out_ve[e_] = I("ve", sVE, lambda e, e_=e_: e.tensor_tensor(
                outb[e_ % 2][:], y16[:, e_ * TQ:(e_ + 1) * TQ],
                scrB[:, 1024:1024 + TQ], op=ALU.add))

            qk = "sy" if e_ % 2 == 0 else "gp"
            W(qk, sVE, out_ve[e_])
            store_ct[e_] = I(qk, sST, lambda e, e_=e_: e.dma_start(
                out=out_t[e_ * 128:(e_ + 1) * 128, :], in_=outb[e_ % 2][:]), 16)

        ff.close()
        ffb.close()
        ffa.close()
        fc_y.close()

        W("sy", sST, store_ct[NCH - 1])
        W("gp", sST, store_ct[NCH - 1])
        W("gp", sGD, sGD.n)
        flush()

    return nc, sorted(dbg.keys())


_BASE = 10000.0


def _tables():
    theta = 1.0 / _BASE ** (np.arange(0, RD, 2, dtype=np.float64) / RD)
    ang = np.arange(L, dtype=np.float64)[:, None] * theta[None, :]
    sin2 = np.repeat(np.sin(ang), 2, axis=1)
    cos2 = np.repeat(np.cos(ang), 2, axis=1)
    tabc = np.concatenate([sin2.T, sin2.T], axis=0).astype(np.float32)
    tabs = np.concatenate([cos2.T, cos2.T], axis=0).astype(np.float32)
    return np.ascontiguousarray(tabc), np.ascontiguousarray(tabs)


def _psgn():
    p = np.zeros((128, 128), np.float32)
    for base in (0, 64):
        for i in range(32):
            p[base + 2 * i + 1, base + 2 * i] = -1.0
            p[base + 2 * i, base + 2 * i + 1] = 1.0
    return p


def _qr_perm():
    idx = []
    for j in range(8):
        idx += list(range((2 * j + 1) * 64, (2 * j + 2) * 64))
        idx += list(range(2 * j * 64, (2 * j + 1) * 64))
    return np.asarray(idx)


def _wfc_e(wfc):
    wt = np.asarray(wfc, dtype=np.float32).T
    a = wt.reshape(8, 128, NCH, 128)
    a = a.transpose(1, 2, 0, 3).reshape(128, NCH * 1024)
    return np.ascontiguousarray(a)


def _pc(b):
    b = np.asarray(b, dtype=np.float32)
    return np.ascontiguousarray(b.reshape(-1, 128).T)


_CACHE = {}


def kernel(**inp):
    f32 = lambda x: np.ascontiguousarray(np.asarray(x, dtype=np.float32))
    h = f32(inp["L0_attn_h"])
    factors = f32(inp["L0_ffn_factors"])

    key = (tuple(sorted(DEBUG)), STOP)
    if key not in _CACHE:
        _CACHE[key] = build_nc(DEBUG, stop=STOP)
    nc, dbg_names = _CACHE[key]

    tabc, tabs = _tables()
    perm = _qr_perm()
    wkrT = f32(np.asarray(inp["L0_attn_W_kr"]).T)
    shared = {
        "wdkvT": f32(np.asarray(inp["L0_attn_W_down_kv"]).T),
        "wdqT": f32(np.asarray(inp["L0_attn_W_down_q"]).T),
        "wkr2T": np.ascontiguousarray(np.concatenate([wkrT, wkrT], axis=1)),
        "wupkT": f32(np.asarray(inp["L0_attn_W_up_k"]).T),
        "wupvT": f32(np.asarray(inp["L0_attn_W_up_v"]).T),
        "wupqT": f32(np.asarray(inp["L0_attn_W_up_q"]).T),
        "wqrTp": f32(np.asarray(inp["L0_attn_W_qr"]).T[:, perm]),
        "wfcTe": _wfc_e(inp["L0_attn_W_fc"]),
        "bdkv": _pc(inp["L0_attn_b_down_kv"]),
        "bdq": _pc(inp["L0_attn_b_down_q"]),
        "bkr2": _pc(np.concatenate([np.asarray(inp["L0_attn_b_kr"]),
                                    np.asarray(inp["L0_attn_b_kr"])])),
        "bupk": _pc(inp["L0_attn_b_up_k"]),
        "bupv": f32(inp["L0_attn_b_up_v"]),
        "bupq": _pc(inp["L0_attn_b_up_q"]),
        "bqrp": _pc(np.asarray(inp["L0_attn_b_qr"])[perm]),
        "bfc": _pc(inp["L0_attn_b_fc"]),
        "bffn": _pc(inp["L0_ffn_bias"]),
        "psgn": _psgn(),
    }

    in_maps = []
    for c in range(8):
        b, qh = c // 2, c % 2
        q0 = qh * TQ
        m = dict(shared)
        m["ht"] = np.ascontiguousarray(np.roll(h[b].T, -q0, axis=1))
        m["ktabc"] = np.ascontiguousarray(np.roll(tabc, -q0, axis=1))
        m["ktabs"] = np.ascontiguousarray(np.roll(tabs, -q0, axis=1))
        m["factc"] = np.ascontiguousarray(factors[c])
        in_maps.append(m)

    res = run_bass_kernel_spmd(nc, in_maps, core_ids=list(range(8)))

    global LAST_DEBUG
    LAST_DEBUG = {}
    if dbg_names:
        LAST_DEBUG = {n: [np.asarray(res.results[c][n]) for c in range(8)]
                      for n in dbg_names}

    out = np.empty((B, L, D), np.float32)
    for c in range(8):
        b, qh = c // 2, c % 2
        q0 = qh * TQ
        if "out_t" in res.results[c]:
            out[b, q0:q0 + TQ, :] = np.asarray(res.results[c]["out_t"]).T
    return out
